# version 11
# baseline (speedup 1.0000x reference)
import numpy as np
import ml_dtypes
import concourse.bass as bass
import concourse.mybir as mybir
from concourse.bass_utils import run_bass_kernel_spmd

F32 = mybir.dt.float32
BF16 = mybir.dt.bfloat16
I32 = mybir.dt.int32
U8 = mybir.dt.uint8
AF = mybir.ActivationFunctionType
ALU = mybir.AluOpType
AX = mybir.AxisListType

B, L, D = 16, 2048, 1024
NCORES = 8
SPC = B // NCORES
T = 512
NMT = L // T
NST = T // 128
NC = D // 128
KD, VD, NH, DK, DV, CH = 512, 1024, 4, 128, 256, 32
GLR, LR, HHID = 64, 32, 3072
NHC = HHID // 128
EPS = 1e-5
INV127 = 1.0 / 127.0
PAIRS = [(0, 1), (0, 2), (0, 3), (1, 2), (1, 3), (2, 3)]
BFD = ml_dtypes.bfloat16


def host_consts():
    t = np.arange(128)
    ch = t // CH
    same = ch[:, None] == ch[None, :]
    m_qe = (same & (t[:, None] < t[None, :]))
    m_ke = (same & (t[:, None] <= t[None, :]))
    m_kw = (same & (t[:, None] > t[None, :]))
    m_qt = (t[:, None] < t[None, :])
    m_ks = (t[:, None] > t[None, :])
    pm = np.zeros((128, 8), np.float32)
    for j, (cp, c) in enumerate(PAIRS):
        pm[:, j] = ((ch > cp) & (ch <= c - 1)).astype(np.float32)
    pm[:, 6] = 1.0
    sel = np.tile((t[:, None] < t[None, :]).astype(np.uint8), (1, NH))
    f32 = np.float32
    return {
        "c_mqe": m_qe.astype(f32), "c_mke": m_ke.astype(f32),
        "c_mkw": m_kw.astype(f32), "c_mqt": m_qt.astype(f32),
        "c_mks": m_ks.astype(f32), "c_pm": pm.astype(f32),
        "c_ident": np.eye(128, dtype=f32), "c_sel": sel,
        "c_ones": np.ones((128, 1), f32),
    }


def host_weights(inp):
    d = {}

    def lhsT_tiles(w):
        o, i = w.shape
        return np.ascontiguousarray(w.T.reshape(i // 128, 128, o).transpose(1, 0, 2))

    d["w_x0"] = lhsT_tiles(np.asarray(inp["L0_attn_x0_w"]))
    d["w_r"] = lhsT_tiles(np.asarray(inp["L0_attn_r_w"]))
    d["w_k"] = lhsT_tiles(np.asarray(inp["L0_attn_k_w"]))
    d["w_v"] = lhsT_tiles(np.asarray(inp["L0_attn_v_w"]))
    d["w_g"] = lhsT_tiles(np.asarray(inp["L0_attn_g_w"]))
    d["w_o"] = lhsT_tiles(np.asarray(inp["L0_attn_o_w"]))
    d["w_w1"] = lhsT_tiles(np.asarray(inp["L0_attn_w_w1"]))
    x2 = np.asarray(inp["L0_attn_x2_w"]).reshape(D, 5, LR)
    aug = np.zeros((5, 33, D), np.float32)
    for n in range(5):
        aug[n, 0:32, :] = x2[:, n, :].T
        aug[n, 32, :] = np.asarray(inp["L0_attn_x_bias"])[n]
    d["w_x2"] = np.ascontiguousarray(aug.transpose(1, 0, 2))
    w2a = np.zeros((65, KD), np.float32)
    w2a[0:64, :] = np.asarray(inp["L0_attn_w_w2"]).T
    w2a[64, :] = np.asarray(inp["L0_attn_w_b2"])
    d["w_w2"] = w2a
    d["w_u"] = np.ascontiguousarray(np.asarray(inp["L0_attn_bonus"]).T)
    d["w_gnw"] = np.ascontiguousarray(
        np.broadcast_to(np.asarray(inp["L0_attn_gn_w"])[None, :], (128, D)))
    d["w_gnb"] = np.ascontiguousarray(
        np.broadcast_to(np.asarray(inp["L0_attn_gn_b"])[None, :], (128, D)))
    d["w_x0mu"] = np.ascontiguousarray(
        np.asarray(inp["L0_attn_x0_mu"]).reshape(NC, 128).T)
    wg = np.asarray(inp["L0_ffn_w_gate"])
    wu = np.asarray(inp["L0_ffn_w_up"])
    wd = np.asarray(inp["L0_ffn_w_down"])
    d["w_fg"] = np.ascontiguousarray(
        wg.reshape(NHC, 128, NC, 128).transpose(0, 3, 2, 1))
    d["w_fu"] = np.ascontiguousarray(
        wu.reshape(NHC, 128, NC, 128).transpose(0, 3, 2, 1))
    d["w_fd"] = np.ascontiguousarray(
        wd.T.reshape(NHC, 128, NC, 128).transpose(2, 1, 0, 3))
    return d


WSHAPES = {
    "w_x0": ([128, NC, 160], F32), "w_r": ([128, NC, KD], F32),
    "w_k": ([128, NC, KD], F32), "w_v": ([128, NC, VD], F32),
    "w_g": ([128, NC, VD], F32), "w_o": ([128, NC, VD], F32),
    "w_w1": ([128, NC, GLR], F32), "w_x2": ([33, 5, D], F32),
    "w_w2": ([65, KD], F32), "w_u": ([128, NH], F32),
    "w_gnw": ([128, D], F32), "w_gnb": ([128, D], F32),
    "w_x0mu": ([128, NC], F32),
    "w_fg": ([NHC, 128, NC, 128], I32), "w_fu": ([NHC, 128, NC, 128], I32),
    "w_fd": ([NC, 128, NHC, 128], I32),
    "c_mqe": ([128, 128], F32), "c_mke": ([128, 128], F32),
    "c_mkw": ([128, 128], F32), "c_mqt": ([128, 128], F32),
    "c_mks": ([128, 128], F32), "c_pm": ([128, 8], F32),
    "c_ident": ([128, 128], F32), "c_sel": ([128, NH * 128], U8),
    "c_ones": ([128, 1], F32),
}


class Ctr:
    def __init__(self, nc, name):
        self.sem = nc.alloc_semaphore(name)
        self.val = 0

    def inc(self, n=1):
        self.val += n
        return self.val


_COMPUTE_OPS = {"activation", "tensor_tensor", "tensor_scalar", "tensor_copy",
                "scalar_tensor_tensor", "tensor_reduce", "reduce_sum",
                "reduce_max", "reciprocal", "select", "copy_predicated",
                "memset", "bn_stats", "bn_aggr", "iota", "tensor_tensor_scan",
                "copy", "mul", "add"}


class Drainer:

    def __init__(self, e):
        self._e = e
        self._dirty = False

    def __getattr__(self, name):
        fn = getattr(self._e, name)
        if name in _COMPUTE_OPS:
            def wrapped(*a, **k):
                if self._dirty:
                    self._e.drain()
                self._dirty = True
                return fn(*a, **k)
            return wrapped
        if name == "drain":
            def wrapped(*a, **k):
                self._dirty = False
                return fn(*a, **k)
            return wrapped
        return fn


class En:
    def __init__(self):
        self.e = None
        self.hw = {}

    def w(self, ctr, val=None):
        v = ctr.val if val is None else val
        if v > 0 and self.hw.get(id(ctr), -1) < v:
            self.e.wait_ge(ctr.sem, v)
            self.hw[id(ctr)] = v


def build_program(n_mt=SPC * NMT, dumps=(), stop=None):
    nc = bass.Bass("TRN2", target_bir_lowering=False, debug=False)

    x_d = nc.dram_tensor("x", [SPC, L, D], F32, kind="ExternalInput").ap()
    out_d = nc.dram_tensor("out", [SPC, L, D], F32, kind="ExternalOutput").ap()
    wdr = {n: nc.dram_tensor(n, shp, dt, kind="ExternalInput").ap()
           for n, (shp, dt) in WSHAPES.items()}
    dbg = {}

    A = nc.alloc_sbuf_tensor
    sb_wr = A("sb_wr", [128, NC, KD], BF16)
    sb_wk = A("sb_wk", [128, NC, KD], BF16)
    sb_wx0 = A("sb_wx0", [128, NC, 160], BF16)
    sb_ww1 = A("sb_ww1", [128, NC, GLR], BF16)
    sb_wx2 = A("sb_wx2", [33, 5, D], BF16)
    sb_ww2 = A("sb_ww2", [65, KD], BF16)
    sb_u = A("sb_u", [128, NH], F32)
    sb_gnw = A("sb_gnw", [128, D], BF16)
    sb_gnb = A("sb_gnb", [128, D], BF16)
    sb_x0mu = A("sb_x0mu", [128, NC], BF16)
    sb_mqe = A("sb_mqe", [128, 128], BF16)
    sb_mke = A("sb_mke", [128, 128], BF16)
    sb_mkw = A("sb_mkw", [128, 128], BF16)
    sb_mqt = A("sb_mqt", [128, 128], BF16)
    sb_mks = A("sb_mks", [128, 128], BF16)
    sb_pm = A("sb_pm", [128, 8], BF16)
    sb_id = A("sb_id", [128, 128], BF16)
    sb_sel = A("sb_sel", [128, NH * 128], U8)
    sb_ones = A("sb_ones", [128, 1], BF16)
    sb_x5 = A("sb_x5", [33, 5, T], BF16)
    sb_h1 = A("sb_h1", [65, T], BF16)
    sb_vgo = A("sb_vgo", [128, NC, VD], BF16)
    sb_fg = A("sb_fg", [128, 2, NC, 128], BF16)
    sb_fu = A("sb_fu", [128, 2, NC, 128], BF16)
    sb_fd = A("sb_fd", [128, 2, NHC, 128], BF16)
    sb_S = A("sb_S", [128, NH, DV], F32)
    sb_Sb = A("sb_Sb", [128, NH, DV], BF16)
    sb_stash = A("sb_stash", [128, NC, 1], BF16)
    sb_x = A("sb_x", [128, 1, D], F32)
    sb_scr = A("sb_scr", [128, D], BF16)
    sb_st1 = A("sb_st1", [128, 8], F32)
    sb_st2 = A("sb_st2", [128, 24], F32)
    sb_xnt = A("sb_xnt", [128, 1, D], BF16)
    sb_xnT = A("sb_xnT", [128, NC, T], BF16)
    sb_dT = A("sb_dT", [128, NC, T], BF16)
    sb_xin = A("sb_xin", [128, 2, NC, T], BF16)
    sb_q = A("sb_q", [128, NH, T], BF16)
    sb_k = A("sb_k", [128, NH, T], BF16)
    sb_whi = A("sb_whi", [128, NH, 128], BF16)
    sb_wlo = A("sb_wlo", [128, NH, 128], BF16)
    sb_v = A("sb_v", [128, NST, NH * DV], BF16)
    sb_gate = A("sb_gate", [128, NST, VD], BF16)
    sb_gff = A("sb_gff", [128, NST, D], BF16)
    sb_hall = A("sb_hall", [128, NHC, T], BF16)
    sb_ex = A("sb_ex", [128, 2, 2 * NH * 128], BF16)
    sb_qe = A("sb_qe", [128, NH, 128], BF16)
    sb_ke = A("sb_ke", [128, NH, 128], BF16)
    sb_kw = A("sb_kw", [128, NH, 128], BF16)
    sb_qt = A("sb_qt", [128, NH, 128], BF16)
    sb_ks = A("sb_ks", [128, NH, 128], BF16)
    sb_ksT = A("sb_ksT", [128, NH, 128], BF16)
    sb_qg = A("sb_qg", [128, NH, 6, 32], BF16)
    sb_mt = A("sb_mt", [128, NH, 128], BF16)
    sb_G = A("sb_G", [128, NH, 8], F32)
    sb_o = A("sb_o", [128, NH * DV], F32)
    sb_z = A("sb_z", [128, VD], BF16)
    sb_gst = A("sb_gst", [128, 32], F32)
    sb_ogT = A("sb_ogT", [128, NC, 128], BF16)
    sb_sg = A("sb_sg", [128, D], F32)
    wf_ap = sb_scr[:].bitcast(F32)
    gsc_ap = sb_scr[:, 0:512]
    qk_ap = None

    qk_ap = sb_qg[:, :, 0:4, :].rearrange("p h j t -> p h (j t)")
    obf = sb_o[:].bitcast(BF16)
    Pab = nc.alloc_psum_tensor("Pab", [128, 1024], F32)
    Pa = Pab[:, 0:512]
    Pb = Pab[:, 512:1024]
    Psm = nc.alloc_psum_tensor("Psm", [128, 512], F32)
    Po = nc.alloc_psum_tensor("Po", [128, 1024], F32)
    Pw = nc.alloc_psum_tensor("Pw", [128, 1024], F32)
    PtrT = nc.alloc_psum_tensor("PtrT", [128, 1024], BF16)
    Ptr = PtrT[:]
    Pmt = PtrT[:].bitcast(F32)

    cx = Ctr(nc, "cx")
    cw = Ctr(nc, "cw")
    cwv = Ctr(nc, "cwv")
    cwg = [Ctr(nc, "cwg0"), Ctr(nc, "cwg1")]
    cwd = [Ctr(nc, "cwd0"), Ctr(nc, "cwd1")]
    ct = Ctr(nc, "ct")
    cv = Ctr(nc, "cv")
    cs = Ctr(nc, "cs")
    co = Ctr(nc, "co")
    cg = Ctr(nc, "cg")
    cdmp = Ctr(nc, "cdmp")
    cxo = Ctr(nc, "cxo")
    ALLC = [cx, cw, cwv, cwg[0], cwg[1], cwd[0], cwd[1], ct, cv, cs, co, cg, cdmp, cxo]

    with nc.Block() as pre:
        @pre.sync
        def _(e):
            for c in ALLC:
                e.sem_clear(c.sem)

    ENS = {n: En() for n in ("sync", "gpsimd", "scalar", "vector", "tensor")}
    BLK = {}

    def on(name, fn):
        def wrap(e):
            if name in ("vector", "scalar", "gpsimd"):
                e = Drainer(e)
            ENS[name].e = e
            fn(e, ENS[name])
        getattr(BLK["b"], name)(wrap)

    def sig_t(e):
        e.drain().then_inc(ct.sem, 1)
        return ct.inc()

    def sig(e, c):
        e.drain().then_inc(c.sem, 1)
        return c.inc()

    def dump(key, sb, ctr, evv):
        if key not in dumps:
            return
        shp = list(sb.shape)
        dbg[key] = nc.dram_tensor("dbg_" + key, shp,
                                  F32, kind="ExternalOutput").ap()

        def pd(e, en):
            en.w(ctr, evv)
            en.w(cdmp)
            e.dma_start(dbg[key], sb[:]).then_inc(cdmp.sem, 16)
            cdmp.inc(16)
        on("gpsimd", pd)
        dval = cdmp.val

        def pwait_v(e, en):
            en.w(cdmp, dval)
        on("vector", pwait_v)

        def pwait_s(e, en):
            en.w(cdmp, dval)
        on("scalar", pwait_s)

        def pwait_t(e, en):
            en.w(cdmp, dval)
        on("tensor", pwait_t)

    ev = {}
    st = {"x_rd": 0, "xnt_rd": 0, "xnT_rd_t": 0, "xnT_rd_v": 0,
          "xin_rd": {}, "vgo_rd": 0, "hall_rd": 0, "gff_rd": 0,
          "ptr_free": 0, "q_rd": 0, "h1_rd": 0, "v_rd": 0, "gate_rd": 0,
          "rwkv_v_tail": 0, "rwkv_s_tail": 0, "rwkv_t_tail": 0,
          "Supd": 0, "x5_rd": 0, "qk_free": 0, "sg_st": 0, "whi_rd": 0,
          "pa_free_s": 0, "pb_free_s": 0, "exp_rd": [0, 0],
          "argA_free": 0, "argB_free": 0, "fac_rd": 0, "z_rd_t": 0}

    def emit_megatile(it):
        s, m = divmod(it, NMT)
        seq0 = (m == 0)
        tok0 = m * T

        def pwv(e, en):
            en.w(ct, st["vgo_rd"])
            e.dma_start(sb_vgo[:], wdr["w_v"]).then_inc(cwv.sem, 16)
            cwv.inc(16)
            ev["vload"] = cwv.val
        on("gpsimd", pwv)

        trv = []
        xev = []
        for t_ in range(NST):
            par = t_ % 2

            def px(e, en, t_=t_):
                en.w(cv, st["x_rd"])
                e.dma_start(sb_x[:, 0, :],
                            x_d[s, tok0 + t_ * 128: tok0 + (t_ + 1) * 128, :]) \
                    .then_inc(cx.sem, 16)
                cx.inc(16)
                xev.append(cx.val)
            on("sync", px)

            def pln_s(e, en, t_=t_, par=par):
                en.w(cx, xev[t_])
                e.activation(out=sb_scr[:, 0:1024], in_=sb_x[:, 0, :],
                             func=AF.Copy, bias=0.0, scale=1.0,
                             accum_out=sb_st1[:, t_:t_ + 1])
                e.drain()
                e.activation(out=sb_scr[:, 0:1024], in_=sb_x[:, 0, :],
                             func=AF.Square,
                             accum_out=sb_st1[:, 4 + t_:5 + t_])
                ev[f"lns{t_}"] = sig(e, cs)
            on("scalar", pln_s)

            def pln_v(e, en, t_=t_):
                en.w(cs, ev[f"lns{t_}"])
                e.tensor_scalar(out=sb_st2[:, t_:t_ + 1], in0=sb_st1[:, t_:t_ + 1],
                                scalar1=1.0 / D, scalar2=None, op0=ALU.mult)
                e.tensor_tensor(out=sb_st2[:, 4 + t_:5 + t_],
                                in0=sb_st2[:, t_:t_ + 1],
                                in1=sb_st2[:, t_:t_ + 1], op=ALU.mult)
                e.tensor_scalar(out=sb_st2[:, 8 + t_:9 + t_],
                                in0=sb_st1[:, 4 + t_:5 + t_],
                                scalar1=1.0 / D, scalar2=EPS, op0=ALU.mult,
                                op1=ALU.add)
                e.tensor_tensor(out=sb_st2[:, 12 + t_:13 + t_],
                                in0=sb_st2[:, 8 + t_:9 + t_],
                                in1=sb_st2[:, 4 + t_:5 + t_], op=ALU.subtract)
                e.reciprocal(out=sb_st2[:, 16 + t_:17 + t_],
                             in_=sb_st2[:, 12 + t_:13 + t_])
                ev[f"lnv{t_}"] = sig(e, cv)
            on("vector", pln_v)

            def pln_s2(e, en, t_=t_):
                en.w(cv, ev[f"lnv{t_}"])
                e.activation(out=sb_st2[:, 20 + t_:21 + t_],
                             in_=sb_st2[:, 16 + t_:17 + t_], func=AF.Sqrt)
                ev[f"lnr{t_}"] = sig(e, cs)
            on("scalar", pln_s2)

            def pln_v2(e, en, t_=t_, par=par):
                en.w(cs, ev[f"lnr{t_}"])
                en.w(ct, st["xnt_rd"])
                e.tensor_scalar(out=sb_xnt[:, 0, :], in0=sb_x[:, 0, :],
                                scalar1=sb_st2[:, t_:t_ + 1], op0=ALU.subtract,
                                scalar2=sb_st2[:, 20 + t_:21 + t_], op1=ALU.mult)
                st["x_rd"] = sig(e, cv)
                ev[f"xnt{t_}"] = st["x_rd"]
            on("vector", pln_v2)

            def ptr_t(e, en, t_=t_, par=par):
                en.w(cv, ev[f"xnt{t_}"])
                en.w(cw, ev["c0"])
                en.w(cv, st["ptr_free"])
                for c in range(NC):
                    e.transpose(Ptr[:, c * 128:(c + 1) * 128],
                                sb_xnt[:, 0, c * 128:(c + 1) * 128], sb_id[:])
                st["xnt_rd"] = sig_t(e)
                ev[f"trt{t_}"] = st["xnt_rd"]
            on("tensor", ptr_t)

            def ptr_v(e, en, t_=t_):
                en.w(ct, ev[f"trt{t_}"])
                if t_ == 0:
                    en.w(ct, st["xnT_rd_t"])
                    en.w(cv, st["xnT_rd_v"])
                e.tensor_copy(out=sb_xnT[:, :, t_ * 128:(t_ + 1) * 128],
                              in_=Ptr.rearrange("p (c t) -> p c t", c=NC))
                st["ptr_free"] = sig(e, cv)
                trv.append(st["ptr_free"])
            on("vector", ptr_v)
        ev["xnT"] = trv[-1]
        dump(f"xnT_{it}", sb_xnT, cv, ev["xnT"])
        if stop == "ln":
            st["xnT_rd_t"] = 0
            st["xnT_rd_v"] = ev["xnT"]
            st["rwkv_v_tail"] = cv.val
            st["rwkv_s_tail"] = cs.val
            return

        dnev = []
        dn_state = {"next": 0}

        def emit_down(kmax):
            while dn_state["next"] < min(kmax, NC):
                dc = dn_state["next"]
                dn_state["next"] += 1
                par = dc % 2

                def pd_ld(e, en, dc=dc, par=par):
                    if dc >= 2:
                        en.w(ct, dnev[dc - 2])
                    e.dma_start(sb_fd[:, par], wdr["w_fd"][dc],
                                max_dma_last_dim=8192).then_inc(cwd[par].sem, 16)
                    cwd[par].inc(16)
                    ev[f"fd{dc}"] = cwd[par].val
                on("gpsimd", pd_ld)

                def pd_t(e, en, dc=dc, par=par):
                    en.w(cwd[dc % 2], ev[f"fd{dc}"])
                    en.w(cv, hev[-1])
                    if dc < 2:
                        en.w(cs, ev[f"silu{NHC-1}"])
                    else:
                        en.w(cs, ev[f"dns{dc-2}"])
                    bank = Pa if dc % 2 == 0 else Pb
                    for hc in range(NHC):
                        e.matmul(bank[:, 0:512], sb_fd[:, par, hc, :],
                                 sb_hall[:, hc, :],
                                 start=(hc == 0), stop=(hc == NHC - 1))
                    dnev.append(sig_t(e))
                on("tensor", pd_t)

                def pd_s(e, en, dc=dc):
                    en.w(ct, dnev[dc])
                    en.w(cv, hev[-1])
                    bank = Pa if dc % 2 == 0 else Pb
                    e.activation(out=gsc_ap, in_=bank[:, 0:512],
                                 func=AF.Silu, bias=0.0, scale=INV127)
                    ev[f"dns{dc}"] = sig(e, cs)
                on("scalar", pd_s)

                def pd_t2(e, en, dc=dc):
                    en.w(cs, ev[f"dns{dc}"])
                    en.w(cv, st["ptr_free"])
                    if dc == 0:
                        en.w(cv, st["gff_rd"])
                    for ts2 in range(NST):
                        e.transpose(Ptr[:, ts2 * 128:(ts2 + 1) * 128],
                                    gsc_ap[:, ts2 * 128:(ts2 + 1) * 128], sb_id[:])
                    ev[f"gft{dc}"] = sig_t(e)
                on("tensor", pd_t2)

                def pd_v(e, en, dc=dc):
                    en.w(ct, ev[f"gft{dc}"])
                    e.tensor_copy(
                        out=sb_gff[:, 0:NST, dc * 128:(dc + 1) * 128],
                        in_=Ptr[:, 0:512].rearrange("p (s d) -> p s d", s=NST))
                    st["ptr_free"] = sig(e, cv)
                    st["gfv_last"] = st["ptr_free"]
                on("vector", pd_v)

        def pmx_v(e, en):
            en.w(cv, ev["xnT"])
            en.w(cw, ev["w0"])
            if seq0:
                e.memset(sb_stash[:], 0.0)
                e.memset(sb_S[:].rearrange("p h v -> p (h v)"), 0.0)
                e.memset(sb_Sb[:].rearrange("p h v -> p (h v)"), 0.0)
            e.tensor_tensor(out=sb_dT[:, :, 0:1], in0=sb_stash[:],
                            in1=sb_xnT[:, :, 0:1], op=ALU.subtract)
            e.tensor_tensor(out=sb_dT[:, :, 1:T], in0=sb_xnT[:, :, 0:T - 1],
                            in1=sb_xnT[:, :, 1:T], op=ALU.subtract)
            e.tensor_copy(out=sb_stash[:], in_=sb_xnT[:, :, T - 1:T])
            en.w(ct, st["xin_rd"].get(4, 0))
            e.tensor_tensor(out=sb_xin[:, 0],
                            in0=sb_dT[:],
                            in1=sb_x0mu[:].unsqueeze(2).broadcast_to([128, NC, T]),
                            op=ALU.mult)
            e.drain()
            e.tensor_tensor(out=sb_xin[:, 0], in0=sb_xin[:, 0], in1=sb_xnT[:],
                            op=ALU.add)
            ev["xin0"] = sig(e, cv)
        on("vector", pmx_v)
        dump(f"dT_{it}", sb_dT, cv, ev["xin0"])

        P4 = [Po[:, 0:512], Po[:, 512:1024], Pw[:, 0:512], Pw[:, 512:1024]]

        def px5_t(e, en):
            en.w(cv, ev["xin0"])
            en.w(cv, st["rwkv_v_tail"])
            for c in range(NC):
                e.matmul(Po[0:128, 0:512], sb_wx0[:, c, 0:128], sb_xin[:, 0, c, :],
                         start=(c == 0), stop=(c == NC - 1))
            for c in range(NC):
                e.matmul(Po[0:32, 512:1024], sb_wx0[:, c, 128:160],
                         sb_xin[:, 0, c, :],
                         start=(c == 0), stop=(c == NC - 1), skip_group_check=True)
            ev["x5t"] = sig_t(e)
        on("tensor", px5_t)

        def px5_s(e, en):
            en.w(ct, ev["x5t"])
            for n in range(5):
                src = Po[32 * n:32 * n + 32, 0:512] if n < 4 else Po[0:32, 512:1024]
                e.activation(out=sb_x5[0:32, n, :], in_=src, func=AF.Tanh)
            ev["x5"] = sig(e, cs)
        on("scalar", px5_s)
        dump(f"x5_{it}", sb_x5, cs, ev["x5"])

        hev = []
        guev = []
        gu_state = {"next": 0}

        def emit_gu(kmax):
            while gu_state["next"] < min(kmax, NHC):
                hc = gu_state["next"]
                gu_state["next"] += 1
                par = hc % 2
                gbank = Pa[:, 0:512] if par == 0 else Psm[:, 0:512]
                ubank = Pb[:, 0:512] if par == 0 else Pmt[:, 0:512]

                def pf_ld(e, en, hc=hc, par=par):
                    if hc >= 2:
                        en.w(ct, guev[hc - 2])
                    e.dma_start(sb_fg[:, par], wdr["w_fg"][hc],
                                max_dma_last_dim=8192).then_inc(cwg[par].sem, 16)
                    e.dma_start(sb_fu[:, par], wdr["w_fu"][hc],
                                max_dma_last_dim=8192).then_inc(cwg[par].sem, 16)
                    cwg[par].inc(32)
                    ev[f"fgu{hc}"] = cwg[par].val
                on("gpsimd", pf_ld)

                def pf_t(e, en, hc=hc, par=par, gbank=gbank, ubank=ubank):
                    en.w(cwg[par], ev[f"fgu{hc}"])
                    en.w(cv, ev["xnT"])
                    if hc < 2:
                        en.w(cv, st["rwkv_v_tail"])
                        en.w(cs, st["rwkv_s_tail"])
                        en.w(cv, st["hall_rd"])
                        en.w(cv, st["ptr_free"])
                    else:
                        en.w(cv, hev[hc - 2])
                        en.w(cs, ev[f"silu{hc-2}"])
                    for c in range(NC):
                        e.matmul(gbank, sb_fg[:, par, c, :], sb_xnT[:, c, :],
                                 start=(c == 0), stop=(c == NC - 1))
                    for c in range(NC):
                        e.matmul(ubank, sb_fu[:, par, c, :], sb_xnT[:, c, :],
                                 start=(c == 0), stop=(c == NC - 1))
                    guev.append(sig_t(e))
                on("tensor", pf_t)

                sh = slice(0, 512) if par == 0 else slice(512, 1024)

                def pf_s(e, en, hc=hc, gbank=gbank, sh=sh):
                    en.w(ct, guev[hc])
                    if hc >= 2:
                        en.w(cv, hev[hc - 2])
                    e.activation(out=sb_scr[:, sh], in_=gbank,
                                 func=AF.Silu, bias=0.0, scale=INV127)
                    ev[f"silu{hc}"] = sig(e, cs)
                on("scalar", pf_s)

                def pf_v(e, en, hc=hc, ubank=ubank, sh=sh):
                    en.w(cs, ev[f"silu{hc}"])
                    e.scalar_tensor_tensor(out=sb_hall[:, hc, :], in0=ubank,
                                           scalar=INV127, in1=sb_scr[:, sh],
                                           op0=ALU.mult, op1=ALU.mult)
                    hev.append(sig(e, cv))
                on("vector", pf_v)

        projev = {}
        GU_PER_PIECE = 2
        gu_quota = [0]

        def tick_gu():
            gu_quota[0] += GU_PER_PIECE
            emit_gu(gu_quota[0])

        for n in range(5):
            par = n % 2
            tick_gu()

            def pmix_tA(e, en, n=n, par=par):
                en.w(cs, ev["x5"])
                if n == 0:
                    en.w(ct, ev["x5t"])
                elif n == 1:
                    en.w(cs, ev["proj0"])
                elif n == 2:
                    en.w(cs, ev["h1"])
                elif n == 3:
                    en.w(cs, ev["proj2"])
                elif n == 4:
                    en.w(cs, ev["pvg3_3"])
                if n >= 2:
                    en.w(ct, projev[n - 2])
                if n >= 1:
                    en.w(cv, ev[f"xinB{n-1}"])
                for c in range(4):
                    e.matmul(P4[c], sb_wx2[:, n, c * 128:(c + 1) * 128],
                             sb_x5[:, n, :], start=True, stop=True,
                             skip_group_check=True)
                ev[f"mixA{n}"] = sig_t(e)
            on("tensor", pmix_tA)

            def pmixA_v(e, en, n=n, par=par):
                en.w(ct, ev[f"mixA{n}"])
                e.tensor_tensor(out=sb_xin[:, par, 0:2, :],
                                in0=Po[:].rearrange("p (c t) -> p c t", c=2),
                                in1=sb_dT[:, 0:2, :], op=ALU.mult)
                e.tensor_tensor(out=sb_xin[:, par, 2:4, :],
                                in0=Pw[:].rearrange("p (c t) -> p c t", c=2),
                                in1=sb_dT[:, 2:4, :], op=ALU.mult)
                ev[f"xinA{n}"] = sig(e, cv)
            on("vector", pmixA_v)
            tick_gu()

            def pmix_tB(e, en, n=n):
                en.w(cv, ev[f"xinA{n}"])
                for c in range(4):
                    e.matmul(P4[c], sb_wx2[:, n, (c + 4) * 128:(c + 5) * 128],
                             sb_x5[:, n, :], start=True, stop=True,
                             skip_group_check=True)
                ev[f"mixB{n}"] = sig_t(e)
            on("tensor", pmix_tB)

            def pmixB_v(e, en, n=n, par=par):
                en.w(ct, ev[f"mixB{n}"])
                e.tensor_tensor(out=sb_xin[:, par, 4:6, :],
                                in0=Po[:].rearrange("p (c t) -> p c t", c=2),
                                in1=sb_dT[:, 4:6, :], op=ALU.mult)
                e.tensor_tensor(out=sb_xin[:, par, 6:8, :],
                                in0=Pw[:].rearrange("p (c t) -> p c t", c=2),
                                in1=sb_dT[:, 6:8, :], op=ALU.mult)
                e.drain()
                e.tensor_tensor(out=sb_xin[:, par].rearrange("p c t -> p (c t)"),
                                in0=sb_xin[:, par].rearrange("p c t -> p (c t)"),
                                in1=sb_xnT[:].rearrange("p c t -> p (c t)"),
                                op=ALU.add)
                ev[f"xinB{n}"] = sig(e, cv)
            on("vector", pmixB_v)
            tick_gu()

            if n in (0, 2):
                wsb = sb_wr if n == 0 else sb_wk
                dst_sb = sb_q if n == 0 else sb_k

                def prj_t(e, en, n=n, par=par, wsb=wsb):
                    en.w(cv, ev[f"xinB{n}"])
                    if n == 0:
                        en.w(cv, st["q_rd"])
                    for kt in range(4):
                        for c in range(NC):
                            e.matmul(P4[kt], wsb[:, c, kt * 128:(kt + 1) * 128],
                                     sb_xin[:, par, c, :],
                                     start=(c == 0), stop=(c == NC - 1),
                                     skip_group_check=True)
                    projev[n] = sig_t(e)
                on("tensor", prj_t)

                def prj_s(e, en, n=n, dst_sb=dst_sb):
                    en.w(ct, projev[n])
                    if n == 0:
                        en.w(cv, st["q_rd"])
                    e.activation(out=dst_sb[:, 0:2, :], in_=Po[:, :],
                                 func=AF.Copy, bias=0.0, scale=1.0)
                    e.activation(out=dst_sb[:, 2:4, :], in_=Pw[:, :],
                                 func=AF.Copy, bias=0.0, scale=1.0)
                    ev[f"proj{n}"] = sig(e, cs)
                    if n == 2:
                        st["qk_free"] = ev[f"proj{n}"]
                on("scalar", prj_s)
            elif n == 1:
                def prj_t(e, en, n=n, par=par):
                    en.w(cv, ev[f"xinB{n}"])
                    en.w(cs, ev["proj0"])
                    for c in range(NC):
                        e.matmul(Po[0:64, 0:512], sb_ww1[:, c, :],
                                 sb_xin[:, par, c, :],
                                 start=(c == 0), stop=(c == NC - 1),
                                 skip_group_check=True)
                    projev[n] = sig_t(e)
                on("tensor", prj_t)

                def prj_s(e, en, n=n):
                    en.w(ct, projev[n])
                    en.w(ct, st["whi_rd"])
                    e.activation(out=sb_h1[0:64, :], in_=Po[0:64, 0:512],
                                 func=AF.Tanh)
                    ev["h1"] = sig(e, cs)
                on("scalar", prj_s)
            else:
                for ts_ in range(NST):
                    def prj_t(e, en, n=n, par=par, ts_=ts_):
                        if ts_ == 0:
                            en.w(cv, ev[f"xinB{n}"])
                            if n == 3:
                                en.w(cwv, ev["vload"])
                                en.w(ct, st["v_rd"])
                            else:
                                en.w(cwv, ev["gload"])
                                en.w(cv, st["gate_rd"])
                        if ts_ >= 2:
                            en.w(cs, ev[f"pvg{n}_{ts_-2}"])
                        bank = Po if ts_ % 2 == 0 else Pw
                        for half in range(2):
                            for c in range(NC):
                                e.matmul(bank[:, half * 512:(half + 1) * 512],
                                         sb_xin[:, par, c,
                                                ts_ * 128:(ts_ + 1) * 128],
                                         sb_vgo[:, c, half * 512:(half + 1) * 512],
                                         start=(c == 0), stop=(c == NC - 1),
                                         skip_group_check=True)
                        ev[f"pvgt{n}_{ts_}"] = sig_t(e)
                    on("tensor", prj_t)

                    def prj_s(e, en, n=n, ts_=ts_):
                        en.w(ct, ev[f"pvgt{n}_{ts_}"])
                        bank = Po if ts_ % 2 == 0 else Pw
                        if n == 3:
                            e.activation(out=sb_v[:, ts_, :], in_=bank[:, :],
                                         func=AF.Copy, bias=0.0, scale=1.0)
                        else:
                            e.activation(out=sb_gate[:, ts_, :], in_=bank[:, :],
                                         func=AF.Silu)
                        ev[f"pvg{n}_{ts_}"] = sig(e, cs)
                    on("scalar", prj_s)
                    if n == 4:
                        emit_down(ts_)
                projev[n] = ev[f"pvgt{n}_{NST-1}"]

                if n == 3:
                    def pwg(e, en):
                        en.w(ct, projev[3])
                        e.dma_start(sb_vgo[:], wdr["w_g"]).then_inc(cwv.sem, 16)
                        cwv.inc(16)
                        ev["gload"] = cwv.val
                    on("gpsimd", pwg)
        emit_gu(NHC)
        emit_down(NC)
        ev["gff"] = st["gfv_last"]
        st["pa_free_s"] = ev["dns6"]
        st["pb_free_s"] = ev["dns7"]
        st["xin_rd"] = {3: projev[3], 4: projev[4]}
        dump(f"gff_{it}", sb_gff, cv, ev["gff"])
        dump(f"q_{it}", sb_q, cs, ev["proj0"])
        dump(f"k_{it}", sb_k, cs, ev["proj2"])
        dump(f"v_{it}", sb_v, cs, ev["pvg3_3"])
        dump(f"gate_{it}", sb_gate, cs, ev["pvg4_3"])

        def pwo(e, en):
            en.w(ct, projev[4])
            e.dma_start(sb_vgo[:], wdr["w_o"]).then_inc(cwv.sem, 16)
            cwv.inc(16)
            ev["oload"] = cwv.val
        on("gpsimd", pwo)

        ppsc = [sb_qe, sb_ke, sb_kw, sb_qt]

        def ppq_v(e, en):
            en.w(cs, st["qk_free"])
            en.w(cw, ev["c0"])
            en.w(ct, st["fac_rd"])
            for pts in range(NST):
                e.tensor_tensor(out=ppsc[pts][:],
                                in0=sb_q[:, :, pts * 128:(pts + 1) * 128],
                                in1=sb_k[:, :, pts * 128:(pts + 1) * 128],
                                op=ALU.mult)
            e.drain()
            for pts in range(NST):
                for h in range(NH):
                    e.tensor_scalar(out=ppsc[pts][:, h, :],
                                    in0=ppsc[pts][:, h, :],
                                    scalar1=sb_u[:, h:h + 1], scalar2=None,
                                    op0=ALU.mult)
            ev["ppq"] = sig(e, cv)
        on("vector", ppq_v)

        def ppd_t(e, en):
            en.w(cv, ev["ppq"])
            en.w(cw, ev["c0"])
            for pts in range(NST):
                for h in range(NH):
                    e.matmul(Psm[:, 40 + 4 * pts + h:41 + 4 * pts + h],
                             ppsc[pts][:, h, :], sb_ones[:],
                             start=True, stop=True, skip_group_check=True)
            ev["ppdall"] = sig_t(e)
        on("tensor", ppd_t)

        rA(it, s, m, 0)
        rB1(it, 0)
        rB2(it, 0)
        for ts_ in range(NST):
            rC1(it, ts_)
            if ts_ + 1 < NST:
                rA(it, s, m, ts_ + 1)
            rC2(it, ts_)
            if ts_ + 1 < NST:
                rB1(it, ts_ + 1)
            rD1(it, ts_)
            if ts_ + 1 < NST:
                rB2(it, ts_ + 1)
            rD2(it, s, m, ts_)

        st["rwkv_v_tail"] = cv.val
        st["rwkv_s_tail"] = cs.val
        st["rwkv_t_tail"] = ct.val
        st["xnT_rd_t"] = guev[-1] if guev else 0
        st["xnT_rd_v"] = ev["xinB4"]
        st["hall_rd"] = dnev[-1]
        st["gff_rd"] = ev["sgv3"]
        st["q_rd"] = ev["fac3_3"]
        st["whi_rd"] = ev["w2t3"]
        st["v_rd"] = ev["ow3"]
        st["gate_rd"] = ev["ogv3"]
        st["vgo_rd"] = ev["sprj3"]

    def rA(it, s, m, ts_):
        def pw2_t(e, en, ts_=ts_):
            en.w(cs, ev["h1"])
            en.w(cs, st["qk_free"])
            en.w(cs, st["pb_free_s"])
            if ts_ > 0:
                en.w(cs, st["argB_free"])
            e.matmul(Pb[:, 0:512], sb_h1[:, ts_ * 128:(ts_ + 1) * 128],
                     sb_ww2[:, :], start=True, stop=True)
            ev[f"w2t{ts_}"] = sig_t(e)
        on("tensor", pw2_t)

        def pw2_s(e, en, ts_=ts_):
            en.w(ct, ev[f"w2t{ts_}"])
            if ts_ == 0:
                en.w(ct, ev[f"gft{NC-1}"])
            else:
                en.w(cv, ev[f"facK_{ts_-1}"])
            e.activation(out=wf_ap, in_=Pb[:, 0:512], func=AF.Exp)
            ev[f"wexp{ts_}"] = sig(e, cs)
        on("scalar", pw2_s)

        def pw2_v(e, en, ts_=ts_):
            en.w(cs, ev[f"wexp{ts_}"])
            if ts_ > 0:
                en.w(ct, ev[f"gt{ts_-1}"])
            e.tensor_copy(out=sb_whi[:].rearrange("p h t -> p (h t)"), in_=wf_ap)
            e.tensor_tensor(out=sb_wlo[:].rearrange("p h t -> p (h t)"),
                            in0=wf_ap,
                            in1=sb_whi[:].rearrange("p h t -> p (h t)"),
                            op=ALU.subtract)
            ev[f"wsp{ts_}"] = sig(e, cv)
        on("vector", pw2_v)
        if ts_ == 0:
            dump(f"whi_{it}", sb_whi, cv, ev["wsp0"])

    def rB1(it, ts_):
        def pG1_t(e, en, ts_=ts_):
            en.w(cv, ev[f"wsp{ts_}"])
            en.w(cv, st["Supd"])
            if ts_ == 0:
                en.w(cs, ev["pvg4_3"])
            for h in range(NH):
                e.matmul(Pw[:, h * 128:(h + 1) * 128], sb_whi[:, h, :],
                         sb_mqe[:], start=True, stop=False,
                         skip_group_check=True)
                e.matmul(Pw[:, h * 128:(h + 1) * 128], sb_wlo[:, h, :],
                         sb_mqe[:], start=False, stop=True,
                         skip_group_check=True)
            for h in range(NH):
                e.matmul(Pw[:, 512 + h * 128:512 + (h + 1) * 128],
                         sb_whi[:, h, :], sb_mkw[:], start=True, stop=False,
                         skip_group_check=True)
                e.matmul(Pw[:, 512 + h * 128:512 + (h + 1) * 128],
                         sb_wlo[:, h, :], sb_mkw[:], start=False, stop=True,
                         skip_group_check=True)
            ev[f"argG1_{ts_}"] = sig_t(e)
        on("tensor", pG1_t)

        def pG1_s(e, en, ts_=ts_):
            en.w(ct, ev[f"argG1_{ts_}"])
            en.w(cv, st["exp_rd"][0])
            e.activation(out=sb_ex[:, 0, :], in_=Pw[:, :], func=AF.Exp,
                         bias=0.0, scale=-1.0)
            ev[f"gx1_{ts_}"] = sig(e, cs)
        on("scalar", pG1_s)

        def pG1_v(e, en, ts_=ts_):
            en.w(cs, ev[f"gx1_{ts_}"])
            en.w(ct, st["fac_rd"])
            if ts_ == 0:
                en.w(ct, ev["ppdall"])
            e.tensor_tensor(out=sb_qe[:],
                            in0=sb_ex[:, 0, 0:512].rearrange(
                                "p (h t) -> p h t", h=NH),
                            in1=sb_q[:, :, ts_ * 128:(ts_ + 1) * 128],
                            op=ALU.mult)
            e.tensor_tensor(out=sb_kw[:],
                            in0=sb_ex[:, 0, 512:1024].rearrange(
                                "p (h t) -> p h t", h=NH),
                            in1=sb_k[:, :, ts_ * 128:(ts_ + 1) * 128],
                            op=ALU.mult)
            st["exp_rd"][0] = sig(e, cv)
            ev[f"fac0_{ts_}"] = st["exp_rd"][0]
            ev[f"fac2_{ts_}"] = st["exp_rd"][0]
        on("vector", pG1_v)

        def pG2_t(e, en, ts_=ts_):
            en.w(cv, ev[f"wsp{ts_}"])
            if ts_ == 0:
                en.w(cs, st["pa_free_s"])
                en.w(cs, st["pb_free_s"])
            else:
                en.w(cs, ev[f"gxK_{ts_-1}"])
            en.w(cs, ev[f"wexp{ts_}"])
            for h in range(NH):
                e.matmul(Pab[:, h * 128:(h + 1) * 128], sb_whi[:, h, :],
                         sb_mqt[:], start=True, stop=False,
                         skip_group_check=True)
                e.matmul(Pab[:, h * 128:(h + 1) * 128], sb_wlo[:, h, :],
                         sb_mqt[:], start=False, stop=True,
                         skip_group_check=True)
            for h in range(NH):
                e.matmul(Pab[:, 512 + h * 128:512 + (h + 1) * 128],
                         sb_whi[:, h, :], sb_mks[:], start=True, stop=False,
                         skip_group_check=True)
                e.matmul(Pab[:, 512 + h * 128:512 + (h + 1) * 128],
                         sb_wlo[:, h, :], sb_mks[:], start=False, stop=True,
                         skip_group_check=True)
            ev[f"argG2_{ts_}"] = sig_t(e)
        on("tensor", pG2_t)

        def pG2_s(e, en, ts_=ts_):
            en.w(ct, ev[f"argG2_{ts_}"])
            en.w(cv, st["exp_rd"][1])
            e.activation(out=sb_ex[:, 1, :], in_=Pab[:, :], func=AF.Exp,
                         bias=0.0, scale=-1.0)
            ev[f"gx2_{ts_}"] = sig(e, cs)
        on("scalar", pG2_s)

        def pG2_v(e, en, ts_=ts_):
            en.w(cs, ev[f"gx2_{ts_}"])
            en.w(ct, st["fac_rd"])
            if ts_ == 0:
                en.w(ct, ev["ppdall"])
            e.tensor_tensor(out=sb_qt[:],
                            in0=sb_ex[:, 1, 0:512].rearrange(
                                "p (h t) -> p h t", h=NH),
                            in1=sb_q[:, :, ts_ * 128:(ts_ + 1) * 128],
                            op=ALU.mult)
            e.tensor_tensor(out=sb_ks[:],
                            in0=sb_ex[:, 1, 512:1024].rearrange(
                                "p (h t) -> p h t", h=NH),
                            in1=sb_k[:, :, ts_ * 128:(ts_ + 1) * 128],
                            op=ALU.mult)
            st["exp_rd"][1] = sig(e, cv)
            ev[f"fac3_{ts_}"] = st["exp_rd"][1]
            ev[f"fac4_{ts_}"] = st["exp_rd"][1]
        on("vector", pG2_v)

        def pK_t(e, en, ts_=ts_):
            en.w(cs, ev[f"gx2_{ts_}"])
            for h in range(NH):
                e.matmul(Pa[:, h * 128:(h + 1) * 128], sb_whi[:, h, :],
                         sb_mke[:], start=True, stop=False,
                         skip_group_check=True)
                e.matmul(Pa[:, h * 128:(h + 1) * 128], sb_wlo[:, h, :],
                         sb_mke[:], start=False, stop=True,
                         skip_group_check=True)
            ev[f"argK_{ts_}"] = sig_t(e)
        on("tensor", pK_t)

        def pK_s(e, en, ts_=ts_):
            en.w(ct, ev[f"argK_{ts_}"])
            e.activation(out=sb_scr[:, 0:512], in_=Pa[:, :], func=AF.Exp,
                         bias=0.0, scale=1.0)
            ev[f"gxK_{ts_}"] = sig(e, cs)
        on("scalar", pK_s)

        def pK_v(e, en, ts_=ts_):
            en.w(cs, ev[f"gxK_{ts_}"])
            e.tensor_tensor(out=sb_ke[:],
                            in0=sb_scr[:, 0:512].rearrange(
                                "p (h t) -> p h t", h=NH),
                            in1=sb_k[:, :, ts_ * 128:(ts_ + 1) * 128],
                            op=ALU.mult)
            ev[f"facK_{ts_}"] = sig(e, cv)
            ev[f"fac1_{ts_}"] = ev[f"facK_{ts_}"]
        on("vector", pK_v)
        st["argA_free"] = ev[f"gxK_{ts_}"]
        st["argB_free"] = ev[f"gx2_{ts_}"]
        if ts_ == 0:
            dump(f"qe_{it}", sb_qe, cv, ev["fac0_0"])
            dump(f"ke_{it}", sb_ke, cv, ev["fac1_0"])
            dump(f"qt_{it}", sb_qt, cv, ev["fac3_0"])
            dump(f"ks_{it}", sb_ks, cv, ev["fac4_0"])

        def pg_t(e, en, ts_=ts_):
            en.w(cv, ev[f"wsp{ts_}"])
            en.w(cv, st.get("psm_rd", 0))
            if ts_ > 0:
                en.w(cs, ev[f"gs{ts_-1}"])
            for h in range(NH):
                e.matmul(Psm[:, h * 8:(h + 1) * 8], sb_whi[:, h, :], sb_pm[:],
                         start=True, stop=False)
                e.matmul(Psm[:, h * 8:(h + 1) * 8], sb_wlo[:, h, :], sb_pm[:],
                         start=False, stop=True)
            ev[f"gt{ts_}"] = sig_t(e)
        on("tensor", pg_t)

        def pg_s(e, en, ts_=ts_):
            en.w(ct, ev[f"gt{ts_}"])
            e.activation(out=sb_G[:].rearrange("p h g -> p (h g)"),
                         in_=Psm[:, 0:32], func=AF.Exp, bias=0.0, scale=-1.0)
            ev[f"gs{ts_}"] = sig(e, cs)
        on("scalar", pg_s)

    def rB2(it, ts_):
        def pqg_v(e, en, ts_=ts_):
            en.w(cs, ev[f"gs{ts_}"])
            en.w(cv, ev[f"fac0_{ts_}"])
            en.w(ct, st.get("qg_rd", 0))
            for h in range(NH):
                for j, (cp, c) in enumerate(PAIRS):
                    e.tensor_scalar(out=sb_qg[:, h, j, :],
                                    in0=sb_qe[:, h, 32 * c:32 * c + 32],
                                    scalar1=sb_G[:, h, j:j + 1], scalar2=None,
                                    op0=ALU.mult)
            ev[f"qg{ts_}"] = sig(e, cv)
        on("vector", pqg_v)

        def pkt_t(e, en, ts_=ts_):
            en.w(cv, ev[f"fac4_{ts_}"])
            en.w(cv, st["ptr_free"])
            en.w(cv, st.get("mt_psum_rd", 0))
            en.w(cs, st["qk_free"])
            for h in range(NH):
                e.transpose(Ptr[:, h * 128:(h + 1) * 128], sb_ks[:, h, :], sb_id[:])
            ev[f"kst{ts_}"] = sig_t(e)
        on("tensor", pkt_t)

        def pkt_v(e, en, ts_=ts_):
            en.w(ct, ev[f"kst{ts_}"])
            en.w(ct, st.get("ksT_rd", 0))
            e.tensor_copy(out=sb_ksT[:].rearrange("p h t -> p (h t)"),
                          in_=Ptr[:, 0:512])
            st["ptr_free"] = sig(e, cv)
            ev[f"ksT{ts_}"] = st["ptr_free"]
        on("vector", pkt_v)

    def rC1(it, ts_):
        def pmt_t(e, en, ts_=ts_):
            en.w(cv, ev[f"fac1_{ts_}"])
            en.w(cv, ev[f"fac2_{ts_}"])
            en.w(cv, ev[f"qg{ts_}"])
            en.w(cv, ev[f"ksT{ts_}"])
            en.w(cv, st.get("mt_psum_rd", 0))
            for h in range(NH):
                for cp in range(4):
                    ops = [("diag", cp)] + [("cross", c) for c in range(cp + 1, 4)]
                    for oi, (kind, c) in enumerate(ops):
                        last = (oi == len(ops) - 1)
                        if kind == "diag":
                            lhs = sb_ke[:, h, 32 * cp:32 * cp + 32]
                            rhs = sb_qe[:, h, 32 * cp:32 * cp + 32]
                        else:
                            j = PAIRS.index((cp, c))
                            lhs = sb_kw[:, h, 32 * cp:32 * cp + 32]
                            rhs = sb_qg[:, h, j, :]
                        e.matmul(Pmt[32 * cp:32 * cp + 32,
                                     h * 128 + 32 * c:h * 128 + 32 * c + 32],
                                 lhs, rhs, start=(oi == 0), stop=last,
                                 tile_position=(0, 32 * cp),
                                 skip_group_check=True)
            ev[f"mtt{ts_}"] = sig_t(e)
            st["qg_rd"] = ev[f"mtt{ts_}"]
        on("tensor", pmt_t)

        def pmt_v(e, en, ts_=ts_):
            en.w(ct, ev[f"mtt{ts_}"])
            en.w(cw, ev["c0"])
            en.w(ct, st.get("mt_sb_rd", 0))
            e.memset(sb_mt[:].rearrange("p h t -> p (h t)"), 0.0)
            e.copy_predicated(out=sb_mt[:].rearrange("p h t -> p (h t)"),
                              mask=sb_sel[:], data=Pmt[:, :])
            ev[f"sel{ts_}"] = sig(e, cv)
            st["mt_psum_rd"] = ev[f"sel{ts_}"]
        on("vector", pmt_v)
        if ts_ == 0:
            dump(f"mt_{it}", sb_mt, cv, ev["sel0"])

    def rC2(it, ts_):
        def po_t(e, en, ts_=ts_):
            en.w(cv, ev[f"sel{ts_}"])
            en.w(cv, ev[f"ksT{ts_}"])
            en.w(cv, ev[f"fac3_{ts_}"])
            en.w(cs, ev[f"pvg3_{ts_}"])
            en.w(cv, st["Supd"])
            en.w(cv, st.get("o_psum_rd", 0))
            en.w(cv, st.get("sg_psum_rd", 0))
            for h in range(NH):
                e.matmul(Po[:, h * 256:(h + 1) * 256], sb_mt[:, h, :],
                         sb_v[:, ts_, h * 256:(h + 1) * 256],
                         start=True, stop=False, skip_group_check=True)
                e.matmul(Po[:, h * 256:(h + 1) * 256], sb_qt[:, h, :],
                         sb_Sb[:, h, :], start=False, stop=True,
                         skip_group_check=True)
            for h in range(NH):
                e.matmul(Pw[:, h * 256:(h + 1) * 256], sb_ksT[:, h, :],
                         sb_v[:, ts_, h * 256:(h + 1) * 256],
                         start=True, stop=True, skip_group_check=True)
            ev[f"ow{ts_}"] = sig_t(e)
            st["mt_sb_rd"] = ev[f"ow{ts_}"]
            st["ksT_rd"] = ev[f"ow{ts_}"]
            st["fac_rd"] = ev[f"ow{ts_}"]
            if ts_ == NST - 1:
                ev["ow3"] = ev[f"ow{ts_}"]
        on("tensor", po_t)

        def pov_v(e, en, ts_=ts_):
            en.w(ct, ev[f"ow{ts_}"])
            en.w(cv, st.get("oz_rd", 0))
            en.w(ct, st.get("z_rd_t", 0))
            for h in range(NH):
                e.tensor_scalar(out=sb_z[:, h * 256:(h + 1) * 256],
                                in0=sb_v[:, ts_, h * 256:(h + 1) * 256],
                                scalar1=Psm[:, 40 + 4 * ts_ + h:
                                            41 + 4 * ts_ + h], scalar2=None,
                                op0=ALU.mult)
            e.tensor_tensor(out=sb_o[:], in0=Po[:, :], in1=sb_z[:],
                            op=ALU.add)
            for h in range(NH):
                e.tensor_scalar(out=sb_S[:, h, :], in0=sb_S[:, h, :],
                                scalar1=sb_G[:, h, 6:7], scalar2=None,
                                op0=ALU.mult)
            e.drain()
            e.tensor_tensor(out=sb_S[:].rearrange("p h v -> p (h v)"),
                            in0=sb_S[:].rearrange("p h v -> p (h v)"),
                            in1=Pw[:, :], op=ALU.add)
            e.tensor_copy(out=sb_Sb[:].rearrange("p h v -> p (h v)"),
                          in_=sb_S[:].rearrange("p h v -> p (h v)"))
            st["Supd"] = sig(e, cv)
            ev[f"osb{ts_}"] = st["Supd"]
            st["o_psum_rd"] = st["Supd"]
            st["psm_rd"] = st["Supd"]
        on("vector", pov_v)
        if ts_ == 0:
            dump(f"o_{it}", sb_o, cv, ev["osb0"])
            dump(f"S_{it}", sb_S, cv, ev["osb0"])

    def rD1(it, ts_):
        def pgn_s(e, en, ts_=ts_):
            en.w(cv, ev[f"osb{ts_}"])
            e.activation(out=sb_z[:], in_=sb_o[:], func=AF.Square)
            ev[f"osq{ts_}"] = sig(e, cs)
        on("scalar", pgn_s)

        def pgn_v(e, en, ts_=ts_):
            en.w(cs, ev[f"osq{ts_}"])
            e.reduce_sum(out=sb_gst[:, 0:4],
                         in_=sb_o[:].rearrange("p (h v) -> p h v", h=NH),
                         axis=AX.X)
            e.reduce_sum(out=sb_gst[:, 4:8],
                         in_=sb_z[:].rearrange("p (h v) -> p h v", h=NH),
                         axis=AX.X)
            e.tensor_scalar(out=sb_gst[:, 8:12], in0=sb_gst[:, 0:4],
                            scalar1=1.0 / DV, scalar2=None, op0=ALU.mult)
            e.tensor_tensor(out=sb_gst[:, 12:16], in0=sb_gst[:, 8:12],
                            in1=sb_gst[:, 8:12], op=ALU.mult)
            e.tensor_scalar(out=sb_gst[:, 16:20], in0=sb_gst[:, 4:8],
                            scalar1=1.0 / DV, scalar2=EPS, op0=ALU.mult,
                            op1=ALU.add)
            e.tensor_tensor(out=sb_gst[:, 20:24], in0=sb_gst[:, 16:20],
                            in1=sb_gst[:, 12:16], op=ALU.subtract)
            e.reciprocal(out=sb_gst[:, 24:28], in_=sb_gst[:, 20:24])
            ev[f"gnv{ts_}"] = sig(e, cv)
        on("vector", pgn_v)

        def pgn_s2(e, en, ts_=ts_):
            en.w(cv, ev[f"gnv{ts_}"])
            e.activation(out=sb_gst[:, 28:32], in_=sb_gst[:, 24:28], func=AF.Sqrt)
            ev[f"gnr{ts_}"] = sig(e, cs)
        on("scalar", pgn_s2)

        def pog_v(e, en, ts_=ts_):
            en.w(cs, ev[f"gnr{ts_}"])
            en.w(cs, ev[f"pvg4_{ts_}"])
            for h in range(NH):
                e.tensor_scalar(out=sb_z[:, h * 256:(h + 1) * 256],
                                in0=sb_o[:, h * 256:(h + 1) * 256],
                                scalar1=sb_gst[:, 8 + h:9 + h], op0=ALU.subtract,
                                scalar2=sb_gst[:, 28 + h:29 + h], op1=ALU.mult)
            e.tensor_tensor(out=obf[:, 0:1024], in0=sb_z[:], in1=sb_gnw[:],
                            op=ALU.mult)
            e.tensor_tensor(out=obf[:, 1024:2048], in0=obf[:, 0:1024],
                            in1=sb_gnb[:], op=ALU.add)
            e.drain()
            e.tensor_tensor(out=sb_z[:], in0=obf[:, 1024:2048],
                            in1=sb_gate[:, ts_, :], op=ALU.mult)
            ev[f"ogv{ts_}"] = sig(e, cv)
            st["oz_rd"] = ev[f"ogv{ts_}"]
            if ts_ == NST - 1:
                ev["ogv3"] = ev[f"ogv{ts_}"]
        on("vector", pog_v)
        if ts_ == 0:
            dump(f"og_{it}", sb_z, cv, ev["ogv0"])

    def rD2(it, s, m, ts_):
        tok0 = m * T + ts_ * 128

        def pot_t(e, en, ts_=ts_):
            en.w(cv, ev[f"ogv{ts_}"])
            en.w(cv, st["ptr_free"])
            for c in range(NC):
                e.transpose(Ptr[:, c * 128:(c + 1) * 128],
                            sb_z[:, c * 128:(c + 1) * 128], sb_id[:])
            ev[f"ogt{ts_}"] = sig_t(e)
            st["z_rd_t"] = ev[f"ogt{ts_}"]
        on("tensor", pot_t)

        def pot_v(e, en, ts_=ts_):
            en.w(ct, ev[f"ogt{ts_}"])
            en.w(ct, st.get("ogT_rd", 0))
            e.tensor_copy(out=sb_ogT[:].rearrange("p c t -> p (c t)"),
                          in_=Ptr[:, :])
            st["ptr_free"] = sig(e, cv)
            ev[f"ogTv{ts_}"] = st["ptr_free"]
        on("vector", pot_v)

        def psp_t(e, en, ts_=ts_):
            en.w(cv, ev[f"ogTv{ts_}"])
            en.w(cwv, ev["oload"])
            en.w(cv, st["Supd"])
            en.w(cv, st.get("sg_psum_rd", 0))
            for half in range(2):
                for c in range(NC):
                    e.matmul(Po[:, half * 512:(half + 1) * 512], sb_ogT[:, c, :],
                             sb_vgo[:, c, half * 512:(half + 1) * 512],
                             start=(c == 0), stop=(c == NC - 1),
                             skip_group_check=True)
            ev[f"sprj{ts_}"] = sig_t(e)
            st["ogT_rd"] = ev[f"sprj{ts_}"]
            if ts_ == NST - 1:
                ev["sprj3"] = ev[f"sprj{ts_}"]
        on("tensor", psp_t)

        def psg_v(e, en, ts_=ts_):
            en.w(ct, ev[f"sprj{ts_}"])
            en.w(cv, ev["gff"])
            en.w(cg, st["sg_st"])
            e.tensor_tensor(out=sb_sg[:], in0=Po[:, :], in1=sb_gff[:, ts_, :],
                            op=ALU.mult)
            ev[f"sgv{ts_}"] = sig(e, cv)
            st["sg_psum_rd"] = ev[f"sgv{ts_}"]
            if ts_ == NST - 1:
                ev["sgv3"] = ev[f"sgv{ts_}"]
        on("vector", psg_v)
        if ts_ == 0:
            dump(f"sg_{it}", sb_sg, cv, ev["sgv0"])

        def pacc_g(e, en, ts_=ts_, s=s, tok0=tok0):
            en.w(cv, ev[f"sgv{ts_}"])
            en.w(cxo, 16)
            en.w(cg)
            e.dma_start(out_d[s, tok0:tok0 + 128, :], sb_sg[:],
                        accum_op=ALU.add).then_inc(cg.sem, 16)
            cg.inc(16)
            st["sg_st"] = cg.val
        on("gpsimd", pacc_g)

    with nc.Block() as b:
        BLK["b"] = b

        def p0_sync(e, en):
            for nm, sb in [("c_sel", sb_sel), ("w_u", sb_u)]:
                e.dma_start(sb[:], wdr[nm]).then_inc(cx.sem, 16)
                cx.inc(16)
            e.dma_start(out_d[:, :, :], x_d[:, :, :]).then_inc(cxo.sem, 16)
            cxo.inc(16)
        on("sync", p0_sync)

        def p0_gp(e, en):
            for nm, sb in [("c_mqe", sb_mqe), ("c_mke", sb_mke), ("c_mkw", sb_mkw),
                           ("c_mqt", sb_mqt), ("c_mks", sb_mks), ("c_pm", sb_pm),
                           ("c_ident", sb_id), ("c_ones", sb_ones),
                           ("w_r", sb_wr), ("w_k", sb_wk), ("w_x0", sb_wx0),
                           ("w_w1", sb_ww1), ("w_x2", sb_wx2), ("w_w2", sb_ww2),
                           ("w_gnw", sb_gnw), ("w_gnb", sb_gnb),
                           ("w_x0mu", sb_x0mu)]:
                e.dma_start(sb[:], wdr[nm]).then_inc(cw.sem, 16)
                cw.inc(16)
        on("gpsimd", p0_gp)
        ev["w0"] = cw.val
        ev["c0"] = cw.val

        def p0_v(e, en):
            e.memset(sb_S[:].rearrange("p h v -> p (h v)"), 0.0)
            e.memset(sb_Sb[:].rearrange("p h v -> p (h v)"), 0.0)
            e.memset(sb_stash[:], 0.0)
            e.memset(sb_x5[32:33, :, :], 1.0)
            e.memset(sb_h1[64:65, :], 1.0)
            sig(e, cv)
        on("vector", p0_v)
        st["rwkv_v_tail"] = cv.val

        for it in range(n_mt):
            emit_megatile(it)

        def pf_sync(e, en):
            en.w(co)
            en.w(cg)
            en.w(cdmp)
        on("sync", pf_sync)

    return nc, dbg


def kernel(**inputs):
    wts = host_weights(inputs)
    cons = host_consts()
    x = np.asarray(inputs["L0_attn_hidden_states"])
    nc, _ = build_program()
    base = {}
    for nm, (shp, dt) in WSHAPES.items():
        src = wts[nm] if nm in wts else cons[nm]
        base[nm] = np.ascontiguousarray(src)
    in_maps = []
    for c in range(NCORES):
        im = dict(base)
        im["x"] = np.ascontiguousarray(x[c * SPC:(c + 1) * SPC])
        in_maps.append(im)
    res = run_bass_kernel_spmd(nc, in_maps, core_ids=list(range(NCORES)))
    out = np.concatenate([r["out"] for r in res.results], axis=0)
    return out.reshape(B, L, D)


# version 12
# speedup vs baseline: 1.2574x; 1.2574x over previous
import numpy as np
import ml_dtypes
import concourse.bass as bass
import concourse.mybir as mybir
from concourse.bass_utils import run_bass_kernel_spmd

F32 = mybir.dt.float32
BF16 = mybir.dt.bfloat16
I32 = mybir.dt.int32
U8 = mybir.dt.uint8
AF = mybir.ActivationFunctionType
ALU = mybir.AluOpType
AX = mybir.AxisListType

B, L, D = 16, 2048, 1024
NCORES = 8
SPC = B // NCORES
T = 512
NMT = L // T
NST = T // 128
NC = D // 128
KD, VD, NH, DK, DV, CH = 512, 1024, 4, 128, 256, 32
GLR, LR, HHID = 64, 32, 3072
NHC = HHID // 128
EPS = 1e-5
INV127 = 1.0 / 127.0
PAIRS = [(0, 1), (0, 2), (0, 3), (1, 2), (1, 3), (2, 3)]
BFD = ml_dtypes.bfloat16


def host_consts():
    t = np.arange(128)
    ch = t // CH
    same = ch[:, None] == ch[None, :]
    m_qe = (same & (t[:, None] < t[None, :]))
    m_ke = (same & (t[:, None] <= t[None, :]))
    m_kw = (same & (t[:, None] > t[None, :]))
    m_qt = (t[:, None] < t[None, :])
    m_ks = (t[:, None] > t[None, :])
    pm = np.zeros((128, 8), np.float32)
    for j, (cp, c) in enumerate(PAIRS):
        pm[:, j] = ((ch > cp) & (ch <= c - 1)).astype(np.float32)
    pm[:, 6] = 1.0
    sel = np.tile((t[:, None] < t[None, :]).astype(np.uint8), (1, NH))
    f32 = np.float32
    return {
        "c_mqe": m_qe.astype(f32), "c_mke": m_ke.astype(f32),
        "c_mkw": m_kw.astype(f32), "c_mqt": m_qt.astype(f32),
        "c_mks": m_ks.astype(f32), "c_pm": pm.astype(f32),
        "c_ident": np.eye(128, dtype=f32), "c_sel": sel,
        "c_ones": np.ones((128, 1), f32),
    }


def host_weights(inp):
    d = {}

    def lhsT_tiles(w):
        o, i = w.shape
        return np.ascontiguousarray(w.T.reshape(i // 128, 128, o).transpose(1, 0, 2))

    d["w_x0"] = lhsT_tiles(np.asarray(inp["L0_attn_x0_w"]))
    d["w_r"] = lhsT_tiles(np.asarray(inp["L0_attn_r_w"]))
    d["w_k"] = lhsT_tiles(np.asarray(inp["L0_attn_k_w"]))
    d["w_v"] = lhsT_tiles(np.asarray(inp["L0_attn_v_w"]))
    d["w_g"] = lhsT_tiles(np.asarray(inp["L0_attn_g_w"]))
    d["w_o"] = lhsT_tiles(np.asarray(inp["L0_attn_o_w"]))
    d["w_w1"] = lhsT_tiles(np.asarray(inp["L0_attn_w_w1"]))
    x2 = np.asarray(inp["L0_attn_x2_w"]).reshape(D, 5, LR)
    aug = np.zeros((5, 33, D), np.float32)
    for n in range(5):
        aug[n, 0:32, :] = x2[:, n, :].T
        aug[n, 32, :] = np.asarray(inp["L0_attn_x_bias"])[n]
    d["w_x2"] = np.ascontiguousarray(aug.transpose(1, 0, 2))
    w2a = np.zeros((65, KD), np.float32)
    w2a[0:64, :] = np.asarray(inp["L0_attn_w_w2"]).T
    w2a[64, :] = np.asarray(inp["L0_attn_w_b2"])
    d["w_w2"] = w2a
    d["w_u"] = np.ascontiguousarray(np.asarray(inp["L0_attn_bonus"]).T)
    d["w_gnw"] = np.ascontiguousarray(
        np.broadcast_to(np.asarray(inp["L0_attn_gn_w"])[None, :], (128, D)))
    d["w_gnb"] = np.ascontiguousarray(
        np.broadcast_to(np.asarray(inp["L0_attn_gn_b"])[None, :], (128, D)))
    d["w_x0mu"] = np.ascontiguousarray(
        np.asarray(inp["L0_attn_x0_mu"]).reshape(NC, 128).T)
    wg = np.asarray(inp["L0_ffn_w_gate"])
    wu = np.asarray(inp["L0_ffn_w_up"])
    wd = np.asarray(inp["L0_ffn_w_down"])
    d["w_fg"] = np.ascontiguousarray(
        wg.reshape(NHC, 128, NC, 128).transpose(0, 3, 2, 1))
    d["w_fu"] = np.ascontiguousarray(
        wu.reshape(NHC, 128, NC, 128).transpose(0, 3, 2, 1))
    d["w_fd"] = np.ascontiguousarray(
        wd.T.reshape(NHC, 128, NC, 128).transpose(2, 1, 0, 3))
    return d


WSHAPES = {
    "w_x0": ([128, NC, 160], F32), "w_r": ([128, NC, KD], F32),
    "w_k": ([128, NC, KD], F32), "w_v": ([128, NC, VD], F32),
    "w_g": ([128, NC, VD], F32), "w_o": ([128, NC, VD], F32),
    "w_w1": ([128, NC, GLR], F32), "w_x2": ([33, 5, D], F32),
    "w_w2": ([65, KD], F32), "w_u": ([128, NH], F32),
    "w_gnw": ([128, D], F32), "w_gnb": ([128, D], F32),
    "w_x0mu": ([128, NC], F32),
    "w_fg": ([NHC, 128, NC, 128], I32), "w_fu": ([NHC, 128, NC, 128], I32),
    "w_fd": ([NC, 128, NHC, 128], I32),
    "c_mqe": ([128, 128], F32), "c_mke": ([128, 128], F32),
    "c_mkw": ([128, 128], F32), "c_mqt": ([128, 128], F32),
    "c_mks": ([128, 128], F32), "c_pm": ([128, 8], F32),
    "c_ident": ([128, 128], F32), "c_sel": ([128, NH * 128], U8),
    "c_ones": ([128, 1], F32),
}


class Ctr:
    def __init__(self, nc, name):
        self.sem = nc.alloc_semaphore(name)
        self.val = 0

    def inc(self, n=1):
        self.val += n
        return self.val


_COMPUTE_OPS = {"activation", "tensor_tensor", "tensor_scalar", "tensor_copy",
                "scalar_tensor_tensor", "tensor_reduce", "reduce_sum",
                "reduce_max", "reciprocal", "select", "copy_predicated",
                "memset", "bn_stats", "bn_aggr", "iota", "tensor_tensor_scan",
                "copy", "mul", "add"}


class Drainer:

    def __init__(self, e):
        self._e = e
        self._dirty = False

    def __getattr__(self, name):
        fn = getattr(self._e, name)
        if name in _COMPUTE_OPS:
            def wrapped(*a, **k):
                if self._dirty:
                    self._e.drain()
                self._dirty = True
                return fn(*a, **k)
            return wrapped
        if name == "drain":
            def wrapped(*a, **k):
                self._dirty = False
                return fn(*a, **k)
            return wrapped
        return fn


class En:
    def __init__(self):
        self.e = None
        self.hw = {}

    def w(self, ctr, val=None):
        v = ctr.val if val is None else val
        if v > 0 and self.hw.get(id(ctr), -1) < v:
            self.e.wait_ge(ctr.sem, v)
            self.hw[id(ctr)] = v


def build_program(n_mt=SPC * NMT, dumps=(), stop=None):
    nc = bass.Bass("TRN2", target_bir_lowering=False, debug=False)

    x_d = nc.dram_tensor("x", [SPC, L, D], F32, kind="ExternalInput").ap()
    out_d = nc.dram_tensor("out", [SPC, L, D], F32, kind="ExternalOutput").ap()
    wdr = {n: nc.dram_tensor(n, shp, dt, kind="ExternalInput").ap()
           for n, (shp, dt) in WSHAPES.items()}
    dbg = {}

    A = nc.alloc_sbuf_tensor
    sb_wr = A("sb_wr", [128, NC, KD], BF16)
    sb_wk = A("sb_wk", [128, NC, KD], BF16)
    sb_wx0 = A("sb_wx0", [128, NC, 160], BF16)
    sb_ww1 = A("sb_ww1", [128, NC, GLR], BF16)
    sb_wx2 = A("sb_wx2", [33, 5, D], BF16)
    sb_ww2 = A("sb_ww2", [65, KD], BF16)
    sb_u = A("sb_u", [128, NH], F32)
    sb_gnw = A("sb_gnw", [128, D], BF16)
    sb_gnb = A("sb_gnb", [128, D], BF16)
    sb_x0mu = A("sb_x0mu", [128, NC], BF16)
    sb_mqe = A("sb_mqe", [128, 128], BF16)
    sb_mke = A("sb_mke", [128, 128], BF16)
    sb_mkw = A("sb_mkw", [128, 128], BF16)
    sb_mqt = A("sb_mqt", [128, 128], BF16)
    sb_mks = A("sb_mks", [128, 128], BF16)
    sb_pm = A("sb_pm", [128, 8], BF16)
    sb_id = A("sb_id", [128, 128], BF16)
    sb_sel = A("sb_sel", [128, NH * 128], U8)
    sb_ones = A("sb_ones", [128, 1], BF16)
    sb_x5 = A("sb_x5", [33, 5, T], BF16)
    sb_h1 = A("sb_h1", [65, T], BF16)
    sb_vgo = A("sb_vgo", [128, NC, VD], BF16)
    sb_fg = A("sb_fg", [128, 2, NC, 128], BF16)
    sb_fu = A("sb_fu", [128, 2, NC, 128], BF16)
    sb_fd = A("sb_fd", [128, 2, NHC, 128], BF16)
    sb_S = A("sb_S", [128, NH, DV], F32)
    sb_Sb = A("sb_Sb", [128, NH, DV], BF16)
    sb_stash = A("sb_stash", [128, NC, 1], BF16)
    sb_x = A("sb_x", [128, 1, D], F32)
    sb_scr = A("sb_scr", [128, D], BF16)
    sb_st1 = A("sb_st1", [128, 8], F32)
    sb_st2 = A("sb_st2", [128, 24], F32)
    sb_xnt = A("sb_xnt", [128, 1, D], BF16)
    sb_xnT = A("sb_xnT", [128, NC, T], BF16)
    sb_dT = A("sb_dT", [128, NC, T], BF16)
    sb_xin = A("sb_xin", [128, 2, NC, T], BF16)
    sb_q = A("sb_q", [128, NH, T], BF16)
    sb_k = A("sb_k", [128, NH, T], BF16)
    sb_whi = A("sb_whi", [128, NH, 128], BF16)
    sb_wlo = A("sb_wlo", [128, NH, 128], BF16)
    sb_v = A("sb_v", [128, NST, NH * DV], BF16)
    sb_gate = A("sb_gate", [128, NST, VD], BF16)
    sb_gff = A("sb_gff", [128, NST, D], BF16)
    sb_hall = A("sb_hall", [128, NHC, T], BF16)
    sb_ex = A("sb_ex", [128, 2, 2 * NH * 128], BF16)
    sb_qe = A("sb_qe", [128, NH, 128], BF16)
    sb_ke = A("sb_ke", [128, NH, 128], BF16)
    sb_kw = A("sb_kw", [128, NH, 128], BF16)
    sb_qt = A("sb_qt", [128, NH, 128], BF16)
    sb_ks = A("sb_ks", [128, NH, 128], BF16)
    sb_ksT = A("sb_ksT", [128, NH, 128], BF16)
    sb_qg = A("sb_qg", [128, NH, 6, 32], BF16)
    sb_mt = A("sb_mt", [128, NH, 128], BF16)
    sb_G = A("sb_G", [128, NH, 8], F32)
    sb_o2 = A("sb_o2", [128, 2, NH * DV], BF16)
    sb_z = A("sb_z", [128, VD], BF16)
    sb_gst = A("sb_gst", [128, 32], F32)
    sb_ogT = A("sb_ogT", [128, NC, 128], BF16)
    sb_sg = A("sb_sg", [128, D], F32)
    wf_ap = sb_scr[:].bitcast(F32)
    gsc_ap = sb_scr[:, 0:512]
    qk_ap = None

    qk_ap = sb_qg[:, :, 0:4, :].rearrange("p h j t -> p h (j t)")
    dv_ap = sb_xnt[:, 0, :]
    Pab = nc.alloc_psum_tensor("Pab", [128, 1024], F32)
    Pa = Pab[:, 0:512]
    Pb = Pab[:, 512:1024]
    Psm = nc.alloc_psum_tensor("Psm", [128, 512], F32)
    Po = nc.alloc_psum_tensor("Po", [128, 1024], F32)
    Pw = nc.alloc_psum_tensor("Pw", [128, 1024], F32)
    PtrT = nc.alloc_psum_tensor("PtrT", [128, 1024], BF16)
    Ptr = PtrT[:]
    Pmt = PtrT[:].bitcast(F32)

    cx = Ctr(nc, "cx")
    cw = Ctr(nc, "cw")
    cwv = Ctr(nc, "cwv")
    cwg = [Ctr(nc, "cwg0"), Ctr(nc, "cwg1")]
    cwd = [Ctr(nc, "cwd0"), Ctr(nc, "cwd1")]
    ct = Ctr(nc, "ct")
    cv = Ctr(nc, "cv")
    cs = Ctr(nc, "cs")
    co = Ctr(nc, "co")
    cg = Ctr(nc, "cg")
    cdmp = Ctr(nc, "cdmp")
    cxo = Ctr(nc, "cxo")
    ALLC = [cx, cw, cwv, cwg[0], cwg[1], cwd[0], cwd[1], ct, cv, cs, co, cg, cdmp, cxo]

    with nc.Block() as pre:
        @pre.sync
        def _(e):
            for c in ALLC:
                e.sem_clear(c.sem)

    ENS = {n: En() for n in ("sync", "gpsimd", "scalar", "vector", "tensor")}
    BLK = {}

    def on(name, fn):
        def wrap(e):
            if name in ("vector", "scalar", "gpsimd"):
                e = Drainer(e)
            ENS[name].e = e
            fn(e, ENS[name])
        getattr(BLK["b"], name)(wrap)

    def sig_t(e):
        e.drain().then_inc(ct.sem, 1)
        return ct.inc()

    def sig(e, c):
        e.drain().then_inc(c.sem, 1)
        return c.inc()

    def dump(key, sb, ctr, evv):
        if key not in dumps:
            return
        shp = list(sb.shape)
        dbg[key] = nc.dram_tensor("dbg_" + key, shp,
                                  F32, kind="ExternalOutput").ap()

        def pd(e, en):
            en.w(ctr, evv)
            en.w(cdmp)
            e.dma_start(dbg[key], sb[:]).then_inc(cdmp.sem, 16)
            cdmp.inc(16)
        on("gpsimd", pd)
        dval = cdmp.val

        def pwait_v(e, en):
            en.w(cdmp, dval)
        on("vector", pwait_v)

        def pwait_s(e, en):
            en.w(cdmp, dval)
        on("scalar", pwait_s)

        def pwait_t(e, en):
            en.w(cdmp, dval)
        on("tensor", pwait_t)

    ev = {}
    st = {"x_rd": 0, "xnt_rd": 0, "xnT_rd_t": 0, "xnT_rd_v": 0,
          "xin_rd": {}, "vgo_rd": 0, "hall_rd": 0, "gff_rd": 0,
          "ptr_free": 0, "q_rd": 0, "h1_rd": 0, "v_rd": 0, "gate_rd": 0,
          "rwkv_v_tail": 0, "rwkv_s_tail": 0, "rwkv_t_tail": 0,
          "Supd": 0, "x5_rd": 0, "qk_free": 0, "sg_st": 0, "whi_rd": 0,
          "sg_psum_rd": 0, "osq_par": [0, 0],
          "pa_free_s": 0, "pb_free_s": 0, "exp_rd": [0, 0],
          "argA_free": 0, "argB_free": 0, "fac_rd": 0, "z_rd_t": 0}

    def emit_megatile(it):
        s, m = divmod(it, NMT)
        seq0 = (m == 0)
        tok0 = m * T

        def pwv(e, en):
            en.w(ct, st["vgo_rd"])
            e.dma_start(sb_vgo[:], wdr["w_v"]).then_inc(cwv.sem, 16)
            cwv.inc(16)
            ev["vload"] = cwv.val
        on("gpsimd", pwv)

        trv = []
        xev = []
        for t_ in range(NST):
            par = t_ % 2

            def px(e, en, t_=t_):
                en.w(cv, st["x_rd"])
                e.dma_start(sb_x[:, 0, :],
                            x_d[s, tok0 + t_ * 128: tok0 + (t_ + 1) * 128, :]) \
                    .then_inc(cx.sem, 16)
                cx.inc(16)
                xev.append(cx.val)
            on("sync", px)

            def pln_s(e, en, t_=t_, par=par):
                en.w(cx, xev[t_])
                e.activation(out=sb_scr[:, 0:1024], in_=sb_x[:, 0, :],
                             func=AF.Copy, bias=0.0, scale=1.0,
                             accum_out=sb_st1[:, t_:t_ + 1])
                e.drain()
                e.activation(out=sb_scr[:, 0:1024], in_=sb_x[:, 0, :],
                             func=AF.Square,
                             accum_out=sb_st1[:, 4 + t_:5 + t_])
                ev[f"lns{t_}"] = sig(e, cs)
            on("scalar", pln_s)

            def pln_v(e, en, t_=t_):
                en.w(cs, ev[f"lns{t_}"])
                e.tensor_scalar(out=sb_st2[:, t_:t_ + 1], in0=sb_st1[:, t_:t_ + 1],
                                scalar1=1.0 / D, scalar2=None, op0=ALU.mult)
                e.tensor_tensor(out=sb_st2[:, 4 + t_:5 + t_],
                                in0=sb_st2[:, t_:t_ + 1],
                                in1=sb_st2[:, t_:t_ + 1], op=ALU.mult)
                e.tensor_scalar(out=sb_st2[:, 8 + t_:9 + t_],
                                in0=sb_st1[:, 4 + t_:5 + t_],
                                scalar1=1.0 / D, scalar2=EPS, op0=ALU.mult,
                                op1=ALU.add)
                e.tensor_tensor(out=sb_st2[:, 12 + t_:13 + t_],
                                in0=sb_st2[:, 8 + t_:9 + t_],
                                in1=sb_st2[:, 4 + t_:5 + t_], op=ALU.subtract)
                e.reciprocal(out=sb_st2[:, 16 + t_:17 + t_],
                             in_=sb_st2[:, 12 + t_:13 + t_])
                ev[f"lnv{t_}"] = sig(e, cv)
            on("vector", pln_v)

            def pln_s2(e, en, t_=t_):
                en.w(cv, ev[f"lnv{t_}"])
                e.activation(out=sb_st2[:, 20 + t_:21 + t_],
                             in_=sb_st2[:, 16 + t_:17 + t_], func=AF.Sqrt)
                ev[f"lnr{t_}"] = sig(e, cs)
            on("scalar", pln_s2)

            def pln_v2(e, en, t_=t_, par=par):
                en.w(cs, ev[f"lnr{t_}"])
                en.w(ct, st["xnt_rd"])
                e.tensor_scalar(out=sb_xnt[:, 0, :], in0=sb_x[:, 0, :],
                                scalar1=sb_st2[:, t_:t_ + 1], op0=ALU.subtract,
                                scalar2=sb_st2[:, 20 + t_:21 + t_], op1=ALU.mult)
                st["x_rd"] = sig(e, cv)
                ev[f"xnt{t_}"] = st["x_rd"]
            on("vector", pln_v2)

            def ptr_t(e, en, t_=t_, par=par):
                en.w(cv, ev[f"xnt{t_}"])
                en.w(cw, ev["c0"])
                en.w(cv, st["ptr_free"])
                for c in range(NC):
                    e.transpose(Ptr[:, c * 128:(c + 1) * 128],
                                sb_xnt[:, 0, c * 128:(c + 1) * 128], sb_id[:])
                st["xnt_rd"] = sig_t(e)
                ev[f"trt{t_}"] = st["xnt_rd"]
            on("tensor", ptr_t)

            def ptr_v(e, en, t_=t_):
                en.w(ct, ev[f"trt{t_}"])
                if t_ == 0:
                    en.w(ct, st["xnT_rd_t"])
                    en.w(cv, st["xnT_rd_v"])
                e.tensor_copy(out=sb_xnT[:, :, t_ * 128:(t_ + 1) * 128],
                              in_=Ptr.rearrange("p (c t) -> p c t", c=NC))
                st["ptr_free"] = sig(e, cv)
                trv.append(st["ptr_free"])
            on("vector", ptr_v)
        ev["xnT"] = trv[-1]
        dump(f"xnT_{it}", sb_xnT, cv, ev["xnT"])
        if stop == "ln":
            st["xnT_rd_t"] = 0
            st["xnT_rd_v"] = ev["xnT"]
            st["rwkv_v_tail"] = cv.val
            st["rwkv_s_tail"] = cs.val
            return

        dnev = []
        dn_state = {"next": 0}

        def emit_down(kmax):
            while dn_state["next"] < min(kmax, NC):
                dc = dn_state["next"]
                dn_state["next"] += 1
                par = dc % 2

                def pd_ld(e, en, dc=dc, par=par):
                    if dc >= 2:
                        en.w(ct, dnev[dc - 2])
                    e.dma_start(sb_fd[:, par], wdr["w_fd"][dc],
                                max_dma_last_dim=8192).then_inc(cwd[par].sem, 16)
                    cwd[par].inc(16)
                    ev[f"fd{dc}"] = cwd[par].val
                on("gpsimd", pd_ld)

                def pd_t(e, en, dc=dc, par=par):
                    en.w(cwd[dc % 2], ev[f"fd{dc}"])
                    en.w(cv, hev[-1])
                    if dc < 2:
                        en.w(cs, ev[f"silu{NHC-1}"])
                    else:
                        en.w(cs, ev[f"dns{dc-2}"])
                    bank = Pa if dc % 2 == 0 else Pb
                    for hc in range(NHC):
                        e.matmul(bank[:, 0:512], sb_fd[:, par, hc, :],
                                 sb_hall[:, hc, :],
                                 start=(hc == 0), stop=(hc == NHC - 1))
                    dnev.append(sig_t(e))
                on("tensor", pd_t)

                def pd_s(e, en, dc=dc):
                    en.w(ct, dnev[dc])
                    en.w(cv, hev[-1])
                    bank = Pa if dc % 2 == 0 else Pb
                    e.activation(out=gsc_ap, in_=bank[:, 0:512],
                                 func=AF.Silu, bias=0.0, scale=INV127)
                    ev[f"dns{dc}"] = sig(e, cs)
                on("scalar", pd_s)

                def pd_t2(e, en, dc=dc):
                    en.w(cs, ev[f"dns{dc}"])
                    en.w(cv, st["ptr_free"])
                    if dc == 0:
                        en.w(cv, st["gff_rd"])
                    for ts2 in range(NST):
                        e.transpose(Ptr[:, ts2 * 128:(ts2 + 1) * 128],
                                    gsc_ap[:, ts2 * 128:(ts2 + 1) * 128], sb_id[:])
                    ev[f"gft{dc}"] = sig_t(e)
                on("tensor", pd_t2)

                def pd_v(e, en, dc=dc):
                    en.w(ct, ev[f"gft{dc}"])
                    e.tensor_copy(
                        out=sb_gff[:, 0:NST, dc * 128:(dc + 1) * 128],
                        in_=Ptr[:, 0:512].rearrange("p (s d) -> p s d", s=NST))
                    st["ptr_free"] = sig(e, cv)
                    st["gfv_last"] = st["ptr_free"]
                on("vector", pd_v)

        def pmx_v(e, en):
            en.w(cv, ev["xnT"])
            en.w(cw, ev["w0"])
            if seq0:
                e.memset(sb_stash[:], 0.0)
                e.memset(sb_S[:].rearrange("p h v -> p (h v)"), 0.0)
                e.memset(sb_Sb[:].rearrange("p h v -> p (h v)"), 0.0)
            e.tensor_tensor(out=sb_dT[:, :, 0:1], in0=sb_stash[:],
                            in1=sb_xnT[:, :, 0:1], op=ALU.subtract)
            e.tensor_tensor(out=sb_dT[:, :, 1:T], in0=sb_xnT[:, :, 0:T - 1],
                            in1=sb_xnT[:, :, 1:T], op=ALU.subtract)
            e.tensor_copy(out=sb_stash[:], in_=sb_xnT[:, :, T - 1:T])
            en.w(ct, st["xin_rd"].get(4, 0))
            e.tensor_tensor(out=sb_xin[:, 0],
                            in0=sb_dT[:],
                            in1=sb_x0mu[:].unsqueeze(2).broadcast_to([128, NC, T]),
                            op=ALU.mult)
            e.drain()
            e.tensor_tensor(out=sb_xin[:, 0], in0=sb_xin[:, 0], in1=sb_xnT[:],
                            op=ALU.add)
            ev["xin0"] = sig(e, cv)
        on("vector", pmx_v)
        dump(f"dT_{it}", sb_dT, cv, ev["xin0"])

        P4 = [Po[:, 0:512], Po[:, 512:1024], Pw[:, 0:512], Pw[:, 512:1024]]

        def px5_t(e, en):
            en.w(cv, ev["xin0"])
            en.w(cv, st["rwkv_v_tail"])
            for c in range(NC):
                e.matmul(Po[0:128, 0:512], sb_wx0[:, c, 0:128], sb_xin[:, 0, c, :],
                         start=(c == 0), stop=(c == NC - 1))
            for c in range(NC):
                e.matmul(Po[0:32, 512:1024], sb_wx0[:, c, 128:160],
                         sb_xin[:, 0, c, :],
                         start=(c == 0), stop=(c == NC - 1), skip_group_check=True)
            ev["x5t"] = sig_t(e)
        on("tensor", px5_t)

        def px5_s(e, en):
            en.w(ct, ev["x5t"])
            for n in range(5):
                src = Po[32 * n:32 * n + 32, 0:512] if n < 4 else Po[0:32, 512:1024]
                e.activation(out=sb_x5[0:32, n, :], in_=src, func=AF.Tanh)
            ev["x5"] = sig(e, cs)
        on("scalar", px5_s)
        dump(f"x5_{it}", sb_x5, cs, ev["x5"])

        hev = []
        guev = []
        gu_state = {"next": 0}

        def emit_gu(kmax):
            while gu_state["next"] < min(kmax, NHC):
                hc = gu_state["next"]
                gu_state["next"] += 1
                par = hc % 2
                gbank = Pa[:, 0:512] if par == 0 else Psm[:, 0:512]
                ubank = Pb[:, 0:512] if par == 0 else Pmt[:, 0:512]

                def pf_ld(e, en, hc=hc, par=par):
                    if hc >= 2:
                        en.w(ct, guev[hc - 2])
                    e.dma_start(sb_fg[:, par], wdr["w_fg"][hc],
                                max_dma_last_dim=8192).then_inc(cwg[par].sem, 16)
                    e.dma_start(sb_fu[:, par], wdr["w_fu"][hc],
                                max_dma_last_dim=8192).then_inc(cwg[par].sem, 16)
                    cwg[par].inc(32)
                    ev[f"fgu{hc}"] = cwg[par].val
                on("gpsimd", pf_ld)

                def pf_t(e, en, hc=hc, par=par, gbank=gbank, ubank=ubank):
                    en.w(cwg[par], ev[f"fgu{hc}"])
                    en.w(cv, ev["xnT"])
                    if hc < 2:
                        en.w(cv, st["rwkv_v_tail"])
                        en.w(cs, st["rwkv_s_tail"])
                        en.w(cv, st["hall_rd"])
                        en.w(cv, st["ptr_free"])
                    else:
                        en.w(cv, hev[hc - 2])
                        en.w(cs, ev[f"silu{hc-2}"])
                    for c in range(NC):
                        e.matmul(gbank, sb_fg[:, par, c, :], sb_xnT[:, c, :],
                                 start=(c == 0), stop=(c == NC - 1))
                    for c in range(NC):
                        e.matmul(ubank, sb_fu[:, par, c, :], sb_xnT[:, c, :],
                                 start=(c == 0), stop=(c == NC - 1))
                    guev.append(sig_t(e))
                on("tensor", pf_t)

                sh = slice(0, 512) if par == 0 else slice(512, 1024)

                def pf_s(e, en, hc=hc, gbank=gbank, sh=sh):
                    en.w(ct, guev[hc])
                    if hc >= 2:
                        en.w(cv, hev[hc - 2])
                    e.activation(out=sb_scr[:, sh], in_=gbank,
                                 func=AF.Silu, bias=0.0, scale=INV127)
                    ev[f"silu{hc}"] = sig(e, cs)
                on("scalar", pf_s)

                def pf_v(e, en, hc=hc, ubank=ubank, sh=sh):
                    en.w(cs, ev[f"silu{hc}"])
                    e.scalar_tensor_tensor(out=sb_hall[:, hc, :], in0=ubank,
                                           scalar=INV127, in1=sb_scr[:, sh],
                                           op0=ALU.mult, op1=ALU.mult)
                    hev.append(sig(e, cv))
                on("vector", pf_v)

        projev = {}
        GU_PER_PIECE = 2
        gu_quota = [0]

        def tick_gu():
            gu_quota[0] += GU_PER_PIECE
            emit_gu(gu_quota[0])

        for n in range(5):
            par = n % 2
            tick_gu()

            def pmix_tA(e, en, n=n, par=par):
                en.w(cs, ev["x5"])
                if n == 0:
                    en.w(ct, ev["x5t"])
                elif n == 1:
                    en.w(cs, ev["proj0"])
                elif n == 2:
                    en.w(cs, ev["h1"])
                elif n == 3:
                    en.w(cs, ev["proj2"])
                elif n == 4:
                    en.w(cs, ev["pvg3_3"])
                if n >= 2:
                    en.w(ct, projev[n - 2])
                if n >= 1:
                    en.w(cv, ev[f"xinB{n-1}"])
                for c in range(4):
                    e.matmul(P4[c], sb_wx2[:, n, c * 128:(c + 1) * 128],
                             sb_x5[:, n, :], start=True, stop=True,
                             skip_group_check=True)
                ev[f"mixA{n}"] = sig_t(e)
            on("tensor", pmix_tA)

            def pmixA_v(e, en, n=n, par=par):
                en.w(ct, ev[f"mixA{n}"])
                e.tensor_tensor(out=sb_xin[:, par, 0:2, :],
                                in0=Po[:].rearrange("p (c t) -> p c t", c=2),
                                in1=sb_dT[:, 0:2, :], op=ALU.mult)
                e.tensor_tensor(out=sb_xin[:, par, 2:4, :],
                                in0=Pw[:].rearrange("p (c t) -> p c t", c=2),
                                in1=sb_dT[:, 2:4, :], op=ALU.mult)
                ev[f"xinA{n}"] = sig(e, cv)
            on("vector", pmixA_v)
            tick_gu()

            def pmix_tB(e, en, n=n):
                en.w(cv, ev[f"xinA{n}"])
                for c in range(4):
                    e.matmul(P4[c], sb_wx2[:, n, (c + 4) * 128:(c + 5) * 128],
                             sb_x5[:, n, :], start=True, stop=True,
                             skip_group_check=True)
                ev[f"mixB{n}"] = sig_t(e)
            on("tensor", pmix_tB)

            def pmixB_v(e, en, n=n, par=par):
                en.w(ct, ev[f"mixB{n}"])
                e.tensor_tensor(out=sb_xin[:, par, 4:6, :],
                                in0=Po[:].rearrange("p (c t) -> p c t", c=2),
                                in1=sb_dT[:, 4:6, :], op=ALU.mult)
                e.tensor_tensor(out=sb_xin[:, par, 6:8, :],
                                in0=Pw[:].rearrange("p (c t) -> p c t", c=2),
                                in1=sb_dT[:, 6:8, :], op=ALU.mult)
                e.drain()
                e.tensor_tensor(out=sb_xin[:, par].rearrange("p c t -> p (c t)"),
                                in0=sb_xin[:, par].rearrange("p c t -> p (c t)"),
                                in1=sb_xnT[:].rearrange("p c t -> p (c t)"),
                                op=ALU.add)
                ev[f"xinB{n}"] = sig(e, cv)
            on("vector", pmixB_v)
            tick_gu()

            if n in (0, 2):
                wsb = sb_wr if n == 0 else sb_wk
                dst_sb = sb_q if n == 0 else sb_k

                def prj_t(e, en, n=n, par=par, wsb=wsb):
                    en.w(cv, ev[f"xinB{n}"])
                    if n == 0:
                        en.w(cv, st["q_rd"])
                    for kt in range(4):
                        for c in range(NC):
                            e.matmul(P4[kt], wsb[:, c, kt * 128:(kt + 1) * 128],
                                     sb_xin[:, par, c, :],
                                     start=(c == 0), stop=(c == NC - 1),
                                     skip_group_check=True)
                    projev[n] = sig_t(e)
                on("tensor", prj_t)

                def prj_s(e, en, n=n, dst_sb=dst_sb):
                    en.w(ct, projev[n])
                    if n == 0:
                        en.w(cv, st["q_rd"])
                    e.activation(out=dst_sb[:, 0:2, :], in_=Po[:, :],
                                 func=AF.Copy, bias=0.0, scale=1.0)
                    e.activation(out=dst_sb[:, 2:4, :], in_=Pw[:, :],
                                 func=AF.Copy, bias=0.0, scale=1.0)
                    ev[f"proj{n}"] = sig(e, cs)
                    if n == 2:
                        st["qk_free"] = ev[f"proj{n}"]
                on("scalar", prj_s)
            elif n == 1:
                def prj_t(e, en, n=n, par=par):
                    en.w(cv, ev[f"xinB{n}"])
                    en.w(cs, ev["proj0"])
                    for c in range(NC):
                        e.matmul(Po[0:64, 0:512], sb_ww1[:, c, :],
                                 sb_xin[:, par, c, :],
                                 start=(c == 0), stop=(c == NC - 1),
                                 skip_group_check=True)
                    projev[n] = sig_t(e)
                on("tensor", prj_t)

                def prj_s(e, en, n=n):
                    en.w(ct, projev[n])
                    en.w(ct, st["whi_rd"])
                    e.activation(out=sb_h1[0:64, :], in_=Po[0:64, 0:512],
                                 func=AF.Tanh)
                    ev["h1"] = sig(e, cs)
                on("scalar", prj_s)
            else:
                for ts_ in range(NST):
                    def prj_t(e, en, n=n, par=par, ts_=ts_):
                        if ts_ == 0:
                            en.w(cv, ev[f"xinB{n}"])
                            if n == 3:
                                en.w(cwv, ev["vload"])
                                en.w(ct, st["v_rd"])
                            else:
                                en.w(cwv, ev["gload"])
                                en.w(cv, st["gate_rd"])
                        if ts_ >= 2:
                            en.w(cs, ev[f"pvg{n}_{ts_-2}"])
                        bank = Po if ts_ % 2 == 0 else Pw
                        for half in range(2):
                            for c in range(NC):
                                e.matmul(bank[:, half * 512:(half + 1) * 512],
                                         sb_xin[:, par, c,
                                                ts_ * 128:(ts_ + 1) * 128],
                                         sb_vgo[:, c, half * 512:(half + 1) * 512],
                                         start=(c == 0), stop=(c == NC - 1),
                                         skip_group_check=True)
                        ev[f"pvgt{n}_{ts_}"] = sig_t(e)
                    on("tensor", prj_t)

                    def prj_s(e, en, n=n, ts_=ts_):
                        en.w(ct, ev[f"pvgt{n}_{ts_}"])
                        bank = Po if ts_ % 2 == 0 else Pw
                        if n == 3:
                            e.activation(out=sb_v[:, ts_, :], in_=bank[:, :],
                                         func=AF.Copy, bias=0.0, scale=1.0)
                        else:
                            e.activation(out=sb_gate[:, ts_, :], in_=bank[:, :],
                                         func=AF.Silu)
                        ev[f"pvg{n}_{ts_}"] = sig(e, cs)
                    on("scalar", prj_s)
                    if n == 4:
                        emit_down(ts_)
                projev[n] = ev[f"pvgt{n}_{NST-1}"]

                if n == 3:
                    def pwg(e, en):
                        en.w(ct, projev[3])
                        e.dma_start(sb_vgo[:], wdr["w_g"]).then_inc(cwv.sem, 16)
                        cwv.inc(16)
                        ev["gload"] = cwv.val
                    on("gpsimd", pwg)
        emit_gu(NHC)
        emit_down(NC)
        ev["gff"] = st["gfv_last"]
        st["pa_free_s"] = ev["dns6"]
        st["pb_free_s"] = ev["dns7"]
        st["xin_rd"] = {3: projev[3], 4: projev[4]}
        dump(f"gff_{it}", sb_gff, cv, ev["gff"])
        dump(f"q_{it}", sb_q, cs, ev["proj0"])
        dump(f"k_{it}", sb_k, cs, ev["proj2"])
        dump(f"v_{it}", sb_v, cs, ev["pvg3_3"])
        dump(f"gate_{it}", sb_gate, cs, ev["pvg4_3"])

        def pwo(e, en):
            en.w(ct, projev[4])
            e.dma_start(sb_vgo[:], wdr["w_o"]).then_inc(cwv.sem, 16)
            cwv.inc(16)
            ev["oload"] = cwv.val
        on("gpsimd", pwo)

        ppsc = [sb_qe, sb_ke, sb_kw, sb_qt]

        def ppq_v(e, en):
            en.w(cs, st["qk_free"])
            en.w(cw, ev["c0"])
            en.w(ct, st["fac_rd"])
            for pts in range(NST):
                op = e.tensor_tensor if pts == 0 else e._e.tensor_tensor
                op(out=ppsc[pts][:],
                   in0=sb_q[:, :, pts * 128:(pts + 1) * 128],
                   in1=sb_k[:, :, pts * 128:(pts + 1) * 128],
                   op=ALU.mult)
            first = [True]
            for pts in range(NST):
                for h in range(NH):
                    op = e.tensor_scalar if first[0] else e._e.tensor_scalar
                    first[0] = False
                    op(out=ppsc[pts][:, h, :],
                       in0=ppsc[pts][:, h, :],
                       scalar1=sb_u[:, h:h + 1], scalar2=None,
                       op0=ALU.mult)
            ev["ppq"] = sig(e, cv)
        on("vector", ppq_v)

        def ppd_t(e, en):
            en.w(cv, ev["ppq"])
            en.w(cw, ev["c0"])
            for pts in range(NST):
                for h in range(NH):
                    e.matmul(Psm[:, 40 + 4 * pts + h:41 + 4 * pts + h],
                             ppsc[pts][:, h, :], sb_ones[:],
                             start=True, stop=True, skip_group_check=True)
            ev["ppdall"] = sig_t(e)
        on("tensor", ppd_t)

        rA(it, s, m, 0)
        rB1(it, 0)
        rB2(it, 0)
        for ts_ in range(NST):
            rC1(it, ts_)
            if ts_ + 1 < NST:
                rA(it, s, m, ts_ + 1)
            rC2(it, ts_)
            if ts_ + 1 < NST:
                rB1(it, ts_ + 1)
                rB2(it, ts_ + 1)
            if ts_ >= 1:
                rD1(it, ts_ - 1)
                rD2(it, s, m, ts_ - 1)
        rD1(it, NST - 1)
        rD2(it, s, m, NST - 1)

        st["rwkv_v_tail"] = cv.val
        st["rwkv_s_tail"] = cs.val
        st["rwkv_t_tail"] = ct.val
        st["xnT_rd_t"] = guev[-1] if guev else 0
        st["xnT_rd_v"] = ev["xinB4"]
        st["hall_rd"] = dnev[-1]
        st["gff_rd"] = ev["sgv3"]
        st["q_rd"] = ev["fac3_3"]
        st["whi_rd"] = ev["w2t3"]
        st["v_rd"] = ev["ow3"]
        st["gate_rd"] = ev["ogv3"]
        st["vgo_rd"] = ev["sprj3"]

    def rA(it, s, m, ts_):
        def pw2_t(e, en, ts_=ts_):
            en.w(cs, ev["h1"])
            en.w(cs, st["qk_free"])
            en.w(cs, st["pb_free_s"])
            if ts_ > 0:
                en.w(cs, st["argB_free"])
            en.w(cv, st["sg_psum_rd"])
            e.matmul(Pb[:, 0:512], sb_h1[:, ts_ * 128:(ts_ + 1) * 128],
                     sb_ww2[:, :], start=True, stop=True)
            ev[f"w2t{ts_}"] = sig_t(e)
        on("tensor", pw2_t)

        def pw2_s(e, en, ts_=ts_):
            en.w(ct, ev[f"w2t{ts_}"])
            if ts_ == 0:
                en.w(ct, ev[f"gft{NC-1}"])
            else:
                en.w(cv, ev[f"facK_{ts_-1}"])
            e.activation(out=wf_ap, in_=Pb[:, 0:512], func=AF.Exp)
            ev[f"wexp{ts_}"] = sig(e, cs)
        on("scalar", pw2_s)

        def pw2_v(e, en, ts_=ts_):
            en.w(cs, ev[f"wexp{ts_}"])
            if ts_ > 0:
                en.w(ct, ev[f"gt{ts_-1}"])
            e.tensor_copy(out=sb_whi[:].rearrange("p h t -> p (h t)"), in_=wf_ap)
            e.tensor_tensor(out=sb_wlo[:].rearrange("p h t -> p (h t)"),
                            in0=wf_ap,
                            in1=sb_whi[:].rearrange("p h t -> p (h t)"),
                            op=ALU.subtract)
            ev[f"wsp{ts_}"] = sig(e, cv)
        on("vector", pw2_v)
        if ts_ == 0:
            dump(f"whi_{it}", sb_whi, cv, ev["wsp0"])

    def rB1(it, ts_):
        def pG1_t(e, en, ts_=ts_):
            en.w(cv, ev[f"wsp{ts_}"])
            en.w(cv, st["Supd"])
            if ts_ == 0:
                en.w(cs, ev["pvg4_3"])
            for h in range(NH):
                e.matmul(Pw[:, h * 128:(h + 1) * 128], sb_whi[:, h, :],
                         sb_mqe[:], start=True, stop=False,
                         skip_group_check=True)
                e.matmul(Pw[:, h * 128:(h + 1) * 128], sb_wlo[:, h, :],
                         sb_mqe[:], start=False, stop=True,
                         skip_group_check=True)
            for h in range(NH):
                e.matmul(Pw[:, 512 + h * 128:512 + (h + 1) * 128],
                         sb_whi[:, h, :], sb_mkw[:], start=True, stop=False,
                         skip_group_check=True)
                e.matmul(Pw[:, 512 + h * 128:512 + (h + 1) * 128],
                         sb_wlo[:, h, :], sb_mkw[:], start=False, stop=True,
                         skip_group_check=True)
            ev[f"argG1_{ts_}"] = sig_t(e)
        on("tensor", pG1_t)

        def pG1_s(e, en, ts_=ts_):
            en.w(ct, ev[f"argG1_{ts_}"])
            en.w(cv, st["exp_rd"][0])
            e.activation(out=sb_ex[:, 0, :], in_=Pw[:, :], func=AF.Exp,
                         bias=0.0, scale=-1.0)
            ev[f"gx1_{ts_}"] = sig(e, cs)
        on("scalar", pG1_s)

        def pG1_v(e, en, ts_=ts_):
            en.w(cs, ev[f"gx1_{ts_}"])
            en.w(ct, st["fac_rd"])
            if ts_ == 0:
                en.w(ct, ev["ppdall"])
            e.tensor_tensor(out=sb_qe[:],
                            in0=sb_ex[:, 0, 0:512].rearrange(
                                "p (h t) -> p h t", h=NH),
                            in1=sb_q[:, :, ts_ * 128:(ts_ + 1) * 128],
                            op=ALU.mult)
            e._e.tensor_tensor(out=sb_kw[:],
                               in0=sb_ex[:, 0, 512:1024].rearrange(
                                   "p (h t) -> p h t", h=NH),
                               in1=sb_k[:, :, ts_ * 128:(ts_ + 1) * 128],
                               op=ALU.mult)
            st["exp_rd"][0] = sig(e, cv)
            ev[f"fac0_{ts_}"] = st["exp_rd"][0]
            ev[f"fac2_{ts_}"] = st["exp_rd"][0]
        on("vector", pG1_v)

        def pG2_t(e, en, ts_=ts_):
            en.w(cv, ev[f"wsp{ts_}"])
            if ts_ == 0:
                en.w(cs, st["pa_free_s"])
                en.w(cs, st["pb_free_s"])
            else:
                en.w(cs, ev[f"gxK_{ts_-1}"])
            en.w(cs, ev[f"wexp{ts_}"])
            en.w(cv, st["sg_psum_rd"])
            for h in range(NH):
                e.matmul(Pab[:, h * 128:(h + 1) * 128], sb_whi[:, h, :],
                         sb_mqt[:], start=True, stop=False,
                         skip_group_check=True)
                e.matmul(Pab[:, h * 128:(h + 1) * 128], sb_wlo[:, h, :],
                         sb_mqt[:], start=False, stop=True,
                         skip_group_check=True)
            for h in range(NH):
                e.matmul(Pab[:, 512 + h * 128:512 + (h + 1) * 128],
                         sb_whi[:, h, :], sb_mks[:], start=True, stop=False,
                         skip_group_check=True)
                e.matmul(Pab[:, 512 + h * 128:512 + (h + 1) * 128],
                         sb_wlo[:, h, :], sb_mks[:], start=False, stop=True,
                         skip_group_check=True)
            ev[f"argG2_{ts_}"] = sig_t(e)
        on("tensor", pG2_t)

        def pG2_s(e, en, ts_=ts_):
            en.w(ct, ev[f"argG2_{ts_}"])
            en.w(cv, st["exp_rd"][1])
            e.activation(out=sb_ex[:, 1, :], in_=Pab[:, :], func=AF.Exp,
                         bias=0.0, scale=-1.0)
            ev[f"gx2_{ts_}"] = sig(e, cs)
        on("scalar", pG2_s)

        def pG2_v(e, en, ts_=ts_):
            en.w(cs, ev[f"gx2_{ts_}"])
            en.w(ct, st["fac_rd"])
            if ts_ == 0:
                en.w(ct, ev["ppdall"])
            e.tensor_tensor(out=sb_qt[:],
                            in0=sb_ex[:, 1, 0:512].rearrange(
                                "p (h t) -> p h t", h=NH),
                            in1=sb_q[:, :, ts_ * 128:(ts_ + 1) * 128],
                            op=ALU.mult)
            e._e.tensor_tensor(out=sb_ks[:],
                               in0=sb_ex[:, 1, 512:1024].rearrange(
                                   "p (h t) -> p h t", h=NH),
                               in1=sb_k[:, :, ts_ * 128:(ts_ + 1) * 128],
                               op=ALU.mult)
            st["exp_rd"][1] = sig(e, cv)
            ev[f"fac3_{ts_}"] = st["exp_rd"][1]
            ev[f"fac4_{ts_}"] = st["exp_rd"][1]
        on("vector", pG2_v)

        def pK_t(e, en, ts_=ts_):
            en.w(cs, ev[f"gx2_{ts_}"])
            for h in range(NH):
                e.matmul(Pa[:, h * 128:(h + 1) * 128], sb_whi[:, h, :],
                         sb_mke[:], start=True, stop=False,
                         skip_group_check=True)
                e.matmul(Pa[:, h * 128:(h + 1) * 128], sb_wlo[:, h, :],
                         sb_mke[:], start=False, stop=True,
                         skip_group_check=True)
            ev[f"argK_{ts_}"] = sig_t(e)
        on("tensor", pK_t)

        def pK_s(e, en, ts_=ts_):
            en.w(ct, ev[f"argK_{ts_}"])
            e.activation(out=sb_scr[:, 0:512], in_=Pa[:, :], func=AF.Exp,
                         bias=0.0, scale=1.0)
            ev[f"gxK_{ts_}"] = sig(e, cs)
        on("scalar", pK_s)

        def pK_v(e, en, ts_=ts_):
            en.w(cs, ev[f"gxK_{ts_}"])
            e.tensor_tensor(out=sb_ke[:],
                            in0=sb_scr[:, 0:512].rearrange(
                                "p (h t) -> p h t", h=NH),
                            in1=sb_k[:, :, ts_ * 128:(ts_ + 1) * 128],
                            op=ALU.mult)
            ev[f"facK_{ts_}"] = sig(e, cv)
            ev[f"fac1_{ts_}"] = ev[f"facK_{ts_}"]
        on("vector", pK_v)
        st["argA_free"] = ev[f"gxK_{ts_}"]
        st["argB_free"] = ev[f"gx2_{ts_}"]
        if ts_ == 0:
            dump(f"qe_{it}", sb_qe, cv, ev["fac0_0"])
            dump(f"ke_{it}", sb_ke, cv, ev["fac1_0"])
            dump(f"qt_{it}", sb_qt, cv, ev["fac3_0"])
            dump(f"ks_{it}", sb_ks, cv, ev["fac4_0"])

        def pg_t(e, en, ts_=ts_):
            en.w(cv, ev[f"wsp{ts_}"])
            en.w(cv, st.get("psm_rd", 0))
            if ts_ > 0:
                en.w(cs, ev[f"gs{ts_-1}"])
            for h in range(NH):
                e.matmul(Psm[:, h * 8:(h + 1) * 8], sb_whi[:, h, :], sb_pm[:],
                         start=True, stop=False)
                e.matmul(Psm[:, h * 8:(h + 1) * 8], sb_wlo[:, h, :], sb_pm[:],
                         start=False, stop=True)
            ev[f"gt{ts_}"] = sig_t(e)
        on("tensor", pg_t)

        def pg_s(e, en, ts_=ts_):
            en.w(ct, ev[f"gt{ts_}"])
            e.activation(out=sb_G[:].rearrange("p h g -> p (h g)"),
                         in_=Psm[:, 0:32], func=AF.Exp, bias=0.0, scale=-1.0)
            ev[f"gs{ts_}"] = sig(e, cs)
        on("scalar", pg_s)

    def rB2(it, ts_):
        def pqg_v(e, en, ts_=ts_):
            en.w(cs, ev[f"gs{ts_}"])
            en.w(cv, ev[f"fac0_{ts_}"])
            en.w(ct, st.get("qg_rd", 0))
            first = [True]
            for h in range(NH):
                for j, (cp, c) in enumerate(PAIRS):
                    op = e.tensor_scalar if first[0] else e._e.tensor_scalar
                    first[0] = False
                    op(out=sb_qg[:, h, j, :],
                       in0=sb_qe[:, h, 32 * c:32 * c + 32],
                       scalar1=sb_G[:, h, j:j + 1], scalar2=None,
                       op0=ALU.mult)
            ev[f"qg{ts_}"] = sig(e, cv)
        on("vector", pqg_v)

        def pkt_t(e, en, ts_=ts_):
            en.w(cv, ev[f"fac4_{ts_}"])
            en.w(cv, st["ptr_free"])
            en.w(cv, st.get("mt_psum_rd", 0))
            en.w(cs, st["qk_free"])
            for h in range(NH):
                e.transpose(Ptr[:, h * 128:(h + 1) * 128], sb_ks[:, h, :], sb_id[:])
            ev[f"kst{ts_}"] = sig_t(e)
        on("tensor", pkt_t)

        def pkt_v(e, en, ts_=ts_):
            en.w(ct, ev[f"kst{ts_}"])
            en.w(ct, st.get("ksT_rd", 0))
            e.tensor_copy(out=sb_ksT[:].rearrange("p h t -> p (h t)"),
                          in_=Ptr[:, 0:512])
            st["ptr_free"] = sig(e, cv)
            ev[f"ksT{ts_}"] = st["ptr_free"]
        on("vector", pkt_v)

    def rC1(it, ts_):
        def pmt_t(e, en, ts_=ts_):
            en.w(cv, ev[f"fac1_{ts_}"])
            en.w(cv, ev[f"fac2_{ts_}"])
            en.w(cv, ev[f"qg{ts_}"])
            en.w(cv, ev[f"ksT{ts_}"])
            en.w(cv, st.get("mt_psum_rd", 0))
            for h in range(NH):
                for cp in range(4):
                    ops = [("diag", cp)] + [("cross", c) for c in range(cp + 1, 4)]
                    for oi, (kind, c) in enumerate(ops):
                        last = (oi == len(ops) - 1)
                        if kind == "diag":
                            lhs = sb_ke[:, h, 32 * cp:32 * cp + 32]
                            rhs = sb_qe[:, h, 32 * cp:32 * cp + 32]
                        else:
                            j = PAIRS.index((cp, c))
                            lhs = sb_kw[:, h, 32 * cp:32 * cp + 32]
                            rhs = sb_qg[:, h, j, :]
                        e.matmul(Pmt[32 * cp:32 * cp + 32,
                                     h * 128 + 32 * c:h * 128 + 32 * c + 32],
                                 lhs, rhs, start=(oi == 0), stop=last,
                                 tile_position=(0, 32 * cp),
                                 skip_group_check=True)
            ev[f"mtt{ts_}"] = sig_t(e)
            st["qg_rd"] = ev[f"mtt{ts_}"]
        on("tensor", pmt_t)

        def pmt_v(e, en, ts_=ts_):
            en.w(ct, ev[f"mtt{ts_}"])
            en.w(cw, ev["c0"])
            en.w(ct, st.get("mt_sb_rd", 0))
            e.memset(sb_mt[:].rearrange("p h t -> p (h t)"), 0.0)
            e.copy_predicated(out=sb_mt[:].rearrange("p h t -> p (h t)"),
                              mask=sb_sel[:], data=Pmt[:, :])
            ev[f"sel{ts_}"] = sig(e, cv)
            st["mt_psum_rd"] = ev[f"sel{ts_}"]
        on("vector", pmt_v)
        if ts_ == 0:
            dump(f"mt_{it}", sb_mt, cv, ev["sel0"])

    def rC2(it, ts_):
        def po_t(e, en, ts_=ts_):
            en.w(cv, ev[f"sel{ts_}"])
            en.w(cv, ev[f"ksT{ts_}"])
            en.w(cv, ev[f"fac3_{ts_}"])
            en.w(cs, ev[f"pvg3_{ts_}"])
            en.w(cv, st["Supd"])
            en.w(cv, st.get("o_psum_rd", 0))
            for h in range(NH):
                e.matmul(Po[:, h * 256:(h + 1) * 256], sb_mt[:, h, :],
                         sb_v[:, ts_, h * 256:(h + 1) * 256],
                         start=True, stop=False, skip_group_check=True)
                e.matmul(Po[:, h * 256:(h + 1) * 256], sb_qt[:, h, :],
                         sb_Sb[:, h, :], start=False, stop=True,
                         skip_group_check=True)
            for h in range(NH):
                e.matmul(Pw[:, h * 256:(h + 1) * 256], sb_ksT[:, h, :],
                         sb_v[:, ts_, h * 256:(h + 1) * 256],
                         start=True, stop=True, skip_group_check=True)
            ev[f"ow{ts_}"] = sig_t(e)
            st["mt_sb_rd"] = ev[f"ow{ts_}"]
            st["ksT_rd"] = ev[f"ow{ts_}"]
            st["fac_rd"] = ev[f"ow{ts_}"]
            if ts_ == NST - 1:
                ev["ow3"] = ev[f"ow{ts_}"]
        on("tensor", po_t)

        def pov_v(e, en, ts_=ts_):
            en.w(ct, ev[f"ow{ts_}"])
            en.w(ct, st["xnt_rd"])
            en.w(cs, st["osq_par"][ts_ % 2])
            for h in range(NH):
                op = e.tensor_scalar if h == 0 else e._e.tensor_scalar
                op(out=dv_ap[:, h * 256:(h + 1) * 256],
                   in0=sb_v[:, ts_, h * 256:(h + 1) * 256],
                   scalar1=Psm[:, 40 + 4 * ts_ + h:41 + 4 * ts_ + h],
                   scalar2=None, op0=ALU.mult)
            e.tensor_tensor(out=sb_o2[:, ts_ % 2, :], in0=Po[:, :], in1=dv_ap,
                            op=ALU.add)
            for h in range(NH):
                op = e.tensor_scalar if h == 0 else e._e.tensor_scalar
                op(out=sb_S[:, h, :], in0=sb_S[:, h, :],
                   scalar1=sb_G[:, h, 6:7], scalar2=None, op0=ALU.mult)
            e.tensor_tensor(out=sb_S[:].rearrange("p h v -> p (h v)"),
                            in0=sb_S[:].rearrange("p h v -> p (h v)"),
                            in1=Pw[:, :], op=ALU.add)
            e.tensor_copy(out=sb_Sb[:].rearrange("p h v -> p (h v)"),
                          in_=sb_S[:].rearrange("p h v -> p (h v)"))
            st["Supd"] = sig(e, cv)
            ev[f"osb{ts_}"] = st["Supd"]
            st["o_psum_rd"] = st["Supd"]
            st["psm_rd"] = st["Supd"]
        on("vector", pov_v)
        if ts_ == 0:
            dump(f"S_{it}", sb_S, cv, ev["osb0"])

    def rD1(it, ts_):
        def pgn_s(e, en, ts_=ts_):
            en.w(cv, ev[f"osb{ts_}"])
            en.w(ct, st.get("z_rd_t", 0))
            e.activation(out=sb_z[:], in_=sb_o2[:, ts_ % 2, :], func=AF.Square)
            ev[f"osq{ts_}"] = sig(e, cs)
        on("scalar", pgn_s)
        st["osq_par"][ts_ % 2] = ev[f"osq{ts_}"]

        def pgn_v(e, en, ts_=ts_):
            en.w(cs, ev[f"osq{ts_}"])
            e.reduce_sum(out=sb_gst[:, 0:4],
                         in_=sb_o2[:, ts_ % 2].rearrange(
                             "p (h v) -> p h v", h=NH),
                         axis=AX.X)
            e._e.reduce_sum(out=sb_gst[:, 4:8],
                            in_=sb_z[:].rearrange("p (h v) -> p h v", h=NH),
                            axis=AX.X)
            e.tensor_scalar(out=sb_gst[:, 8:12], in0=sb_gst[:, 0:4],
                            scalar1=1.0 / DV, scalar2=None, op0=ALU.mult)
            e.tensor_tensor(out=sb_gst[:, 12:16], in0=sb_gst[:, 8:12],
                            in1=sb_gst[:, 8:12], op=ALU.mult)
            e.tensor_scalar(out=sb_gst[:, 16:20], in0=sb_gst[:, 4:8],
                            scalar1=1.0 / DV, scalar2=EPS, op0=ALU.mult,
                            op1=ALU.add)
            e.tensor_tensor(out=sb_gst[:, 20:24], in0=sb_gst[:, 16:20],
                            in1=sb_gst[:, 12:16], op=ALU.subtract)
            e.reciprocal(out=sb_gst[:, 24:28], in_=sb_gst[:, 20:24])
            ev[f"gnv{ts_}"] = sig(e, cv)
        on("vector", pgn_v)

        def pgn_s2(e, en, ts_=ts_):
            en.w(cv, ev[f"gnv{ts_}"])
            e.activation(out=sb_gst[:, 28:32], in_=sb_gst[:, 24:28], func=AF.Sqrt)
            ev[f"gnr{ts_}"] = sig(e, cs)
        on("scalar", pgn_s2)

        def pog_v(e, en, ts_=ts_):
            en.w(cs, ev[f"gnr{ts_}"])
            en.w(cs, ev[f"pvg4_{ts_}"])
            for h in range(NH):
                op = e.tensor_scalar if h == 0 else e._e.tensor_scalar
                op(out=sb_z[:, h * 256:(h + 1) * 256],
                   in0=sb_o2[:, ts_ % 2, h * 256:(h + 1) * 256],
                   scalar1=sb_gst[:, 8 + h:9 + h], op0=ALU.subtract,
                   scalar2=sb_gst[:, 28 + h:29 + h], op1=ALU.mult)
            e.tensor_tensor(out=sb_z[:], in0=sb_z[:], in1=sb_gnw[:],
                            op=ALU.mult)
            e.tensor_tensor(out=sb_z[:], in0=sb_z[:], in1=sb_gnb[:],
                            op=ALU.add)
            e.tensor_tensor(out=sb_z[:], in0=sb_z[:],
                            in1=sb_gate[:, ts_, :], op=ALU.mult)
            ev[f"ogv{ts_}"] = sig(e, cv)
            if ts_ == NST - 1:
                ev["ogv3"] = ev[f"ogv{ts_}"]
        on("vector", pog_v)
        if ts_ == 0:
            dump(f"og_{it}", sb_z, cv, ev["ogv0"])

    def rD2(it, s, m, ts_):
        tok0 = m * T + ts_ * 128

        def pot_t(e, en, ts_=ts_):
            en.w(cv, ev[f"ogv{ts_}"])
            en.w(cv, st["ptr_free"])
            for c in range(NC):
                e.transpose(Ptr[:, c * 128:(c + 1) * 128],
                            sb_z[:, c * 128:(c + 1) * 128], sb_id[:])
            ev[f"ogt{ts_}"] = sig_t(e)
            st["z_rd_t"] = ev[f"ogt{ts_}"]
        on("tensor", pot_t)

        def pot_v(e, en, ts_=ts_):
            en.w(ct, ev[f"ogt{ts_}"])
            en.w(ct, st.get("ogT_rd", 0))
            e.tensor_copy(out=sb_ogT[:].rearrange("p c t -> p (c t)"),
                          in_=Ptr[:, :])
            st["ptr_free"] = sig(e, cv)
            ev[f"ogTv{ts_}"] = st["ptr_free"]
        on("vector", pot_v)

        def psp_t(e, en, ts_=ts_):
            en.w(cv, ev[f"ogTv{ts_}"])
            en.w(cwv, ev["oload"])
            en.w(cs, ev[f"gxK_{min(ts_ + 2, NST - 1)}"])
            en.w(cv, st.get("sg_psum_rd", 0))
            for half in range(2):
                for c in range(NC):
                    e.matmul(Pab[:, half * 512:(half + 1) * 512],
                             sb_ogT[:, c, :],
                             sb_vgo[:, c, half * 512:(half + 1) * 512],
                             start=(c == 0), stop=(c == NC - 1),
                             skip_group_check=True)
            ev[f"sprj{ts_}"] = sig_t(e)
            st["ogT_rd"] = ev[f"sprj{ts_}"]
            if ts_ == NST - 1:
                ev["sprj3"] = ev[f"sprj{ts_}"]
        on("tensor", psp_t)

        def psg_v(e, en, ts_=ts_):
            en.w(ct, ev[f"sprj{ts_}"])
            en.w(cv, ev["gff"])
            en.w(cg, st["sg_st"])
            e.tensor_tensor(out=sb_sg[:], in0=Pab[:, :], in1=sb_gff[:, ts_, :],
                            op=ALU.mult)
            ev[f"sgv{ts_}"] = sig(e, cv)
            st["sg_psum_rd"] = ev[f"sgv{ts_}"]
            if ts_ == NST - 1:
                ev["sgv3"] = ev[f"sgv{ts_}"]
        on("vector", psg_v)
        if ts_ == 0:
            dump(f"sg_{it}", sb_sg, cv, ev["sgv0"])

        def pacc_g(e, en, ts_=ts_, s=s, tok0=tok0):
            en.w(cv, ev[f"sgv{ts_}"])
            en.w(cxo, 16)
            en.w(cg)
            e.dma_start(out_d[s, tok0:tok0 + 128, :], sb_sg[:],
                        accum_op=ALU.add).then_inc(cg.sem, 16)
            cg.inc(16)
            st["sg_st"] = cg.val
        on("gpsimd", pacc_g)

    with nc.Block() as b:
        BLK["b"] = b

        def p0_sync(e, en):
            for nm, sb in [("c_sel", sb_sel), ("w_u", sb_u)]:
                e.dma_start(sb[:], wdr[nm]).then_inc(cx.sem, 16)
                cx.inc(16)
            e.dma_start(out_d[:, :, :], x_d[:, :, :]).then_inc(cxo.sem, 16)
            cxo.inc(16)
        on("sync", p0_sync)

        def p0_gp(e, en):
            for nm, sb in [("c_mqe", sb_mqe), ("c_mke", sb_mke), ("c_mkw", sb_mkw),
                           ("c_mqt", sb_mqt), ("c_mks", sb_mks), ("c_pm", sb_pm),
                           ("c_ident", sb_id), ("c_ones", sb_ones),
                           ("w_r", sb_wr), ("w_k", sb_wk), ("w_x0", sb_wx0),
                           ("w_w1", sb_ww1), ("w_x2", sb_wx2), ("w_w2", sb_ww2),
                           ("w_gnw", sb_gnw), ("w_gnb", sb_gnb),
                           ("w_x0mu", sb_x0mu)]:
                e.dma_start(sb[:], wdr[nm]).then_inc(cw.sem, 16)
                cw.inc(16)
        on("gpsimd", p0_gp)
        ev["w0"] = cw.val
        ev["c0"] = cw.val

        def p0_v(e, en):
            e.memset(sb_S[:].rearrange("p h v -> p (h v)"), 0.0)
            e.memset(sb_Sb[:].rearrange("p h v -> p (h v)"), 0.0)
            e.memset(sb_stash[:], 0.0)
            e.memset(sb_x5[32:33, :, :], 1.0)
            e.memset(sb_h1[64:65, :], 1.0)
            sig(e, cv)
        on("vector", p0_v)
        st["rwkv_v_tail"] = cv.val

        for it in range(n_mt):
            emit_megatile(it)

        def pf_sync(e, en):
            en.w(co)
            en.w(cg)
            en.w(cdmp)
        on("sync", pf_sync)

    return nc, dbg


def kernel(**inputs):
    wts = host_weights(inputs)
    cons = host_consts()
    x = np.asarray(inputs["L0_attn_hidden_states"])
    nc, _ = build_program()
    base = {}
    for nm, (shp, dt) in WSHAPES.items():
        src = wts[nm] if nm in wts else cons[nm]
        base[nm] = np.ascontiguousarray(src)
    in_maps = []
    for c in range(NCORES):
        im = dict(base)
        im["x"] = np.ascontiguousarray(x[c * SPC:(c + 1) * SPC])
        in_maps.append(im)
    res = run_bass_kernel_spmd(nc, in_maps, core_ids=list(range(NCORES)))
    out = np.concatenate([r["out"] for r in res.results], axis=0)
    return out.reshape(B, L, D)
